# version 22
# baseline (speedup 1.0000x reference)
import os
import numpy as np
import ml_dtypes
import concourse.bass as bass
import concourse.mybir as mybir
from concourse.bass_utils import run_bass_kernel_spmd
from concourse.alu_op_type import AluOpType as ALU

F32 = mybir.dt.float32
BF16 = mybir.dt.bfloat16
AF = mybir.ActivationFunctionType

D = 2048
S = 2048
H = 16
DK = 128
SEG = 512
NSEG = S // SEG
NT = S // 128
ND = D // 128
EPS = 1e-5
ISCALE = 1.0 / float(np.sqrt(np.float32(DK)))

PHASES = os.environ.get("MK_PHASES", "ABC")
DEV = os.environ.get("MK_DEV", "0") == "1"


class Rec:
    engines = ("tensor", "vector", "scalar", "sync", "gpsimd")

    def __init__(self):
        self.q = {e: [] for e in self.engines}

    def em(self, eng, fn):
        self.q[eng].append(fn)

    def wait(self, eng, sem, val):
        if val <= 0:
            return
        assert val <= sem.n, f"wait target {val} > recorded {sem.n} on {sem.name}"
        self.q[eng].append(lambda e, s=sem.h, v=val: e.wait_ge(s, v))

    def inc(self, eng, sem, fn, k=1):
        self.q[eng].append(lambda e, f=fn, s=sem.h, kk=k: f(e).then_inc(s, kk))
        sem.n += k
        return sem.n

    def run(self, nc):
        with nc.Block() as blk:
            for e in self.engines:
                if self.q[e]:
                    getattr(blk, e)(lambda eng, fns=tuple(self.q[e]): [f(eng) for f in fns])
        self.q = {e: [] for e in self.engines}


class Sem:
    def __init__(self, nc, name):
        self.h = nc.alloc_semaphore(name)
        self.n = 0
        self.name = name


def build_program(phases=PHASES, dev=DEV):
    nc = bass.Bass("TRN2", target_bir_lowering=False, debug=False)

    x_h = nc.dram_tensor("x", [S, D], F32, kind="ExternalInput").ap()
    wq_h = nc.dram_tensor("wq", [D, D], F32, kind="ExternalInput").ap()
    wk_h = nc.dram_tensor("wk", [D, D], F32, kind="ExternalInput").ap()
    wv_h = nc.dram_tensor("wv", [D, D], F32, kind="ExternalInput").ap()
    wo_h = nc.dram_tensor("wo", [D, D], F32, kind="ExternalInput").ap()
    wf_h = nc.dram_tensor("wf", [D, D], F32, kind="ExternalInput").ap()
    ident_h = nc.dram_tensor("ident", [128, 128], BF16, kind="ExternalInput").ap()
    trimask_h = nc.dram_tensor("trimask", [128, 128], BF16, kind="ExternalInput").ap()
    betab_h = nc.dram_tensor("betab", [128, D], F32, kind="ExternalInput").ap()
    biasb_h = nc.dram_tensor("biasb", [128, D], F32, kind="ExternalInput").ap()
    fscale_h = nc.dram_tensor("fscale", [128, ND], F32, kind="ExternalInput").ap()

    spill_kind = "ExternalOutput" if (dev and "C" not in phases) else "Internal"
    qT_sp = nc.dram_tensor("qT_sp", [NSEG, D, SEG], BF16, kind=spill_kind).ap()
    kT_sp = nc.dram_tensor("kT_sp", [NSEG, D, SEG], BF16, kind=spill_kind).ap()
    v_sp = nc.dram_tensor("v_sp", [NSEG, SEG, D], BF16, kind=spill_kind).ap()
    attT_kind = "ExternalOutput" if (dev and "C" not in phases and "B" in phases) else "Internal"
    attT_sp = nc.dram_tensor("attT_sp", [NSEG, D, SEG], BF16, kind=attT_kind).ap()
    out_h = None
    if "C" in phases:
        out_h = nc.dram_tensor("out", [S, D], F32, kind="ExternalOutput").ap()

    sb = nc.alloc_sbuf_tensor
    BIG0 = sb("BIG0", [128, 32768], BF16)
    BIGA = sb("BIGA", [128, 32768], BF16)
    x_sb = sb("x_sb", [128, 4 * D], F32)
    stage = sb("stage", [128, 8 * 512], BF16)
    betab = sb("betab_sb", [128, D], F32)
    fscale = sb("fscale_sb", [128, ND], F32)
    h_sb = sb("h_sb", [128, D], BF16)
    M_f32 = sb("M_f32", [128, H * 130], F32)
    M_bf16 = sb("M_bf16", [128, 2 * 130], BF16)
    sigqT = sb("sigqT", [128, 2 * SEG], BF16)
    sigkT = sb("sigkT", [128, 2 * SEG], BF16)
    sigkN = sb("sigkN", [128, SEG], BF16)
    attN = sb("attN", [128, SEG], BF16)
    delta = sb("delta", [128, 4 * 130], BF16)
    amn = sb("amn", [128, SEG], F32)
    adn = sb("adn", [128, SEG], F32)
    sigtmp = sb("sigtmp", [128, SEG], BF16)
    sigtmp2 = sb("sigtmp2", [128, SEG], BF16)
    sigtmp3 = sb("sigtmp3", [128, SEG], BF16)
    dtmp = sb("dtmp", [128, SEG], BF16)
    ident = sb("ident_sb", [128, 128], BF16)
    trimask = sb("trimask_sb", [128, 128], BF16)
    bnbuf = sb("bnbuf", [128, 24], F32)
    bnout = sb("bnout", [128, 2], F32)
    rstd = sb("rstd", [128, 1], F32)
    negmr = sb("negmr", [128, 1], F32)
    lntmp = sb("lntmp", [128, 1], F32)
    rcol = sb("rcol", [128, 12], F32)

    hT = BIG0
    TRI = (0, 512, 896, 1152)
    expS = BIG0[:, 28800:28800 + 1280]
    biasr = BIG0[:, 30080:30080 + 2048]
    wT = BIG0
    Wo_sb = BIGA
    qT_off, kT_off = 0, 8192
    vext_off = 0
    attT_off = 16 * 520

    pf = [nc.alloc_psum_tensor(f"pf{i}", [128, 512], F32) for i in range(6)]
    pgA = nc.alloc_psum_tensor("pgA", [128, 1024], BF16)
    pgB = nc.alloc_psum_tensor("pgB", [128, 1024], BF16)

    sPE = Sem(nc, "sPE")
    sV = Sem(nc, "sV")
    sS = Sem(nc, "sS")
    sSy = Sem(nc, "sSy")
    sGp = Sem(nc, "sGp")
    sOut = Sem(nc, "sOut")
    sWbL = [Sem(nc, "sWbL0"), Sem(nc, "sWbL1")]

    rec = Rec()

    for dst, src in ((ident, ident_h), (trimask, trimask_h), (betab, betab_h),
                     (fscale, fscale_h)):
        rec.inc("sync", sSy, lambda e, d=dst, s=src: e.dma_start(out=d[:], in_=s), 16)
    consts_ld = sSy.n
    bias_pend = [True]
    rec.wait("scalar", sSy, consts_ld)
    rec.inc("scalar", sS, lambda e: e.activation(betab[:], betab[:], AF.Exp, scale=-1.0))
    rec.wait("vector", sS, sS.n)
    rec.inc("vector", sV, lambda e: e.tensor_scalar_add(betab[:], betab[:], 1.0))
    rec.wait("vector", sV, sV.n)
    rec.inc("vector", sV, lambda e: e.reciprocal(betab[:], betab[:]))
    beta_done = sV.n
    rec.inc("vector", sV, lambda e: e.memset(M_f32[:], 0.0))
    rec.wait("vector", sV, sV.n)
    for h in range(H):
        rec.inc("vector", sV, lambda e, c=130 * h + 128: e.memset(M_f32[:, c:c + 1], 1.0 / 128.0))
        rec.inc("vector", sV, lambda e, c=130 * h + 129: e.memset(M_f32[:, c:c + 1], -1.0 / 128.0))
    rec.wait("scalar", sV, sV.n)
    m_ready = sS.n

    ln_prev_norm = [0]

    def ln_tile(src_ap, dst_ap, ready_sem, ready_val, extra_scalar_waits=()):
        rec.wait("vector", ready_sem, ready_val)
        rec.wait("vector", sS, ln_prev_norm[0])
        for c in range(4):
            fn = lambda e, cc=c, s=src_ap: e.bn_stats(bnbuf[:, 6 * cc:6 * cc + 6], s[:, 512 * cc:512 * cc + 512])
            if c == 3:
                rec.inc("vector", sV, fn)
            else:
                rec.em("vector", fn)
        rec.wait("vector", sV, sV.n)
        rec.inc("vector", sV, lambda e: e.bn_aggr(bnout[:], bnbuf[:]))
        rec.wait("vector", sV, sV.n)
        rec.inc("vector", sV, lambda e: e.tensor_scalar_add(lntmp[:], bnout[:, 1:2], EPS))
        rec.wait("scalar", sV, sV.n)
        rec.em("scalar", lambda e: e.activation(negmr[:], lntmp[:], AF.Ln))
        rec.inc("scalar", sS, lambda e: e.activation(rstd[:], negmr[:], AF.Exp, scale=-0.5))
        rec.wait("vector", sS, sS.n)
        rec.inc("vector", sV, lambda e: e.scalar_tensor_tensor(negmr[:], bnout[:, 0:1], -1.0, rstd[:], op0=ALU.mult, op1=ALU.mult))
        rec.wait("scalar", sV, sV.n)
        for sem, val in extra_scalar_waits:
            rec.wait("scalar", sem, val)
        rec.inc("scalar", sS, lambda e, s=src_ap, d=dst_ap: e.activation(d, s, AF.Identity, bias=negmr[:], scale=rstd[:]))
        ln_prev_norm[0] = sS.n
        return sS.n

    slot_last_dma = [0] * 8
    spill_ctr = [0]

    def spill_psum(bank_ap, dest_ap, pe_val, eng):
        g = spill_ctr[0]
        spill_ctr[0] += 1
        slot = g % 8
        st_ap = stage[:, 512 * slot:512 * slot + 512]
        sem = sS if eng == "scalar" else sV
        rec.wait(eng, sPE, pe_val)
        rec.wait(eng, sSy, slot_last_dma[slot])
        if eng == "scalar":
            rec.inc("scalar", sS, lambda e, b=bank_ap, st=st_ap: e.copy(st, b))
        else:
            rec.inc("vector", sV, lambda e, b=bank_ap, st=st_ap: e.tensor_copy(st, b))
        ev_val = sem.n
        rec.wait("sync", sem, ev_val)
        rec.inc("sync", sSy, lambda e, d=dest_ap, st=st_ap: e.dma_start(out=d, in_=st), 16)
        slot_last_dma[slot] = sSy.n
        return ev_val

    if "A" in phases:
        def emit_w_half_load(w_ap, hf, buf, gate_val=0):
            rec.wait("gpsimd", sPE, gate_val)
            for b4 in range(4):
                def f(e, w=w_ap, b4=b4, hh=hf, bb=buf):
                    src = w[512 * b4:512 * (b4 + 1), 1024 * hh:1024 * (hh + 1)]
                    src = src.rearrange("(k p) c -> p k c", p=128)
                    dst = BIGA[:, 16384 * bb + 4096 * b4:16384 * bb + 4096 * (b4 + 1)]
                    dst = dst.rearrange("p (k c) -> p k c", c=1024)
                    return e.dma_start(out=dst, in_=src)
                rec.inc("gpsimd", sGp, f, 16)
            return sGp.n

        ln_scal_done = [0, 0]
        tr_done = [0, 0]
        ev_done = [0, 0]
        for i in range(NT):
            xa = x_sb[:, (i % 2) * D:(i % 2) * D + D]
            rec.wait("sync", sS, ln_scal_done[i % 2])
            if i == 0:
                rec.wait("sync", sV, beta_done)
            rec.inc("sync", sSy, lambda e, xs=xa, ii=i: e.dma_start(out=xs, in_=x_h[128 * ii:128 * (ii + 1), :]), 16)
            ln_scal_done[i % 2] = ln_tile(xa, h_sb[:], sSy, sSy.n, extra_scalar_waits=((sPE, tr_done[1]),))
            rec.wait("tensor", sS, ln_scal_done[i % 2])
            rec.wait("tensor", sV, ev_done[0])
            rec.wait("tensor", sS, ev_done[1])
            for k in range(ND):
                pg = pgA if k < 8 else pgB
                co = 128 * (k % 8)
                fn = lambda e, kk=k, p=pg, c=co: e.transpose(p[:, c:c + 128], h_sb[:, 128 * kk:128 * (kk + 1)], ident[:])
                if k in (7, 15):
                    rec.inc("tensor", sPE, fn)
                    tr_done[0 if k == 7 else 1] = sPE.n
                else:
                    rec.em("tensor", fn)
            def evA(e, ii=i):
                src = pgA[:].rearrange("p (k c) -> p k c", c=128)
                dst = hT[:].rearrange("p (k s) -> p k s", s=2048)[:, 0:8, 128 * ii:128 * ii + 128]
                return e.tensor_copy(dst, src)
            def evB(e, ii=i):
                src = pgB[:].rearrange("p (k c) -> p k c", c=128)
                dst = hT[:].rearrange("p (k s) -> p k s", s=2048)[:, 8:16, 128 * ii:128 * ii + 128]
                return e.copy(dst, src)
            rec.wait("vector", sPE, tr_done[0])
            ev_done[0] = rec.inc("vector", sV, evA)
            rec.wait("scalar", sPE, tr_done[1])
            ev_done[1] = rec.inc("scalar", sS, evB)
        hT_ready_v, hT_ready_s = ev_done

        load_done = [0] * 6
        mm_done_gate = [0] * 6
        bank_ev = [0, 0]
        gctr = [0]
        first_group = [True]

        def mm_group_prelude(g, w_target):
            bank = pf[g % 2]
            eng = "scalar" if g % 2 == 0 else "vector"
            sem = sS if g % 2 == 0 else sV
            rec.wait("tensor", sGp, w_target)
            if first_group[0]:
                rec.wait("tensor", sV, hT_ready_v)
                rec.wait("tensor", sS, hT_ready_s)
                first_group[0] = False
            rec.wait("tensor", sem, bank_ev[g % 2])
            return bank, eng

        def emit_qk(widx, w_ap, dest_sp):
            for hf in range(2):
                li = widx * 2 + hf
                load_done[li] = emit_w_half_load(w_ap, hf, hf, mm_done_gate[li])
                for m in range(8):
                    for c in range(NSEG):
                        g = gctr[0]; gctr[0] += 1
                        bank, eng = mm_group_prelude(g, load_done[li])
                        for k in range(ND):
                            fn = (lambda e, b=bank, hh=hf, mm=m, cc=c, kk=k:
                                  e.matmul(b[:],
                                           BIGA[:, 16384 * hh + 1024 * kk + 128 * mm:16384 * hh + 1024 * kk + 128 * mm + 128],
                                           hT[:, 2048 * kk + 512 * cc:2048 * kk + 512 * cc + 512],
                                           start=(kk == 0), stop=(kk == ND - 1)))
                            if k == ND - 1:
                                rec.inc("tensor", sPE, fn)
                            else:
                                rec.em("tensor", fn)
                        mrow = 128 * (8 * hf + m)
                        bank_ev[g % 2] = spill_psum(bank[:], dest_sp[c, mrow:mrow + 128, :], sPE.n, eng)
                nli = li + 2
                if nli < 6:
                    mm_done_gate[nli] = sPE.n

        def emit_v(w_ap, dest_sp):
            for hf in range(2):
                li = 4 + hf
                load_done[li] = emit_w_half_load(w_ap, hf, hf, mm_done_gate[li])
                for st_ in range(NT):
                    for c2 in range(2):
                        g = gctr[0]; gctr[0] += 1
                        bank, eng = mm_group_prelude(g, load_done[li])
                        for k in range(ND):
                            fn = (lambda e, b=bank, hh=hf, ss=st_, cc=c2, kk=k:
                                  e.matmul(b[:],
                                           hT[:, 2048 * kk + 128 * ss:2048 * kk + 128 * ss + 128],
                                           BIGA[:, 16384 * hh + 1024 * kk + 512 * cc:16384 * hh + 1024 * kk + 512 * cc + 512],
                                           start=(kk == 0), stop=(kk == ND - 1)))
                            if k == ND - 1:
                                rec.inc("tensor", sPE, fn)
                            else:
                                rec.em("tensor", fn)
                        seg, srow = st_ // 4, 128 * (st_ % 4)
                        dcol = 1024 * hf + 512 * c2
                        bank_ev[g % 2] = spill_psum(bank[:], dest_sp[seg, srow:srow + 128, dcol:dcol + 512], sPE.n, eng)

        emit_qk(0, wq_h, qT_sp)
        emit_qk(1, wk_h, kT_sp)
        emit_v(wv_h, v_sp)
        rec.wait("sync", sSy, sSy.n)
        rec.run(nc)

    if "B" in phases:
        rec.inc("gpsimd", sGp, lambda e: e.dma_start(out=biasr[0:1, :], in_=biasb_h[0:1, :]), 16)
        bias_ld = sGp.n
        h2T_off = 16512
        wtq_off = 24704
        woq_off = (16384, 24576)

        wo_bf = nc.dram_tensor("wo_bf", [D, D], BF16, kind="Internal").ap()
        wtT_bf = nc.dram_tensor("wtT_bf", [D, D], BF16, kind="Internal").ap()

        rec.inc("gpsimd", sGp, lambda e: e.memset(BIG0[:, 0:16 * 520], 0.0))
        rec.wait("gpsimd", sGp, sGp.n)
        for ht in range(16 * 4):
            rec.inc("gpsimd", sGp, lambda e, c=130 * ht + 128: e.memset(BIG0[:, c:c + 1], 1.0))
        rec.inc("gpsimd", sGp, lambda e: e.memset(delta[:], 0.0))
        rec.wait("gpsimd", sGp, sGp.n)
        for i in range(4):
            rec.inc("gpsimd", sGp, lambda e, c=130 * i + 128: e.memset(delta[:, c:c + 1], 1.0))
            rec.inc("gpsimd", sGp, lambda e, c=130 * i + 129: e.memset(delta[:, c:c + 1], -1.0))
        vext_init = sGp.n

        wb_scale_done = 0
        wb_tr = [0, 0]
        wb_ev = [0, 0]
        wb_stage_dma = 0
        wb_ld = [0, 0]
        rec.wait("sync", sGp, vext_init)

        def _ldq0(e):
            src_ = qT_sp[0].rearrange("(h p) s -> p h s", p=128)
            dst = BIGA[:, 0:8192].rearrange("p (h s) -> p h s", s=512)
            return e.dma_start(out=dst, in_=src_)

        def _ldk0(e):
            src_ = kT_sp[0].rearrange("(h p) s -> p h s", p=128)
            dst = BIGA[:, 8192:16384].rearrange("p (h s) -> p h s", s=512)
            return e.dma_start(out=dst, in_=src_)

        def _ldv0(e):
            src_ = v_sp[0].rearrange("(t p) (h v) -> p h t v", p=128, v=128)
            dst = BIG0[:, 0:16 * 520].rearrange("p (h t c) -> p h t c", t=4, c=130)[:, :, :, 0:128]
            return e.dma_start(out=dst, in_=src_)
        rec.inc("sync", sSy, _ldq0, 16)
        rec.inc("sync", sSy, _ldk0, 16)
        rec.inc("sync", sSy, _ldv0, 16)
        seg0_ld_val = [sSy.n]
        for k in range(2):
            rec.inc("gpsimd", sWbL[k % 2], lambda e, kk=k: e.dma_start(
                out=x_sb[:, (2 + kk % 2) * D:(2 + kk % 2) * D + D],
                in_=wf_h[128 * kk:128 * (kk + 1), :]), 16)
            wb_ld[k] = sWbL[k % 2].n
        for k in range(ND):
            rec.wait("scalar", sWbL[k % 2], wb_ld[k % 2])
            rec.wait("scalar", sPE, wb_tr[1])
            rec.inc("scalar", sS, lambda e, kk=k: e.activation(
                h_sb[:], x_sb[:, (2 + kk % 2) * D:(2 + kk % 2) * D + D], AF.Identity,
                scale=fscale[:, kk:kk + 1]))
            wb_scale_done = sS.n

            rec.wait("tensor", sS, wb_scale_done)
            rec.wait("tensor", sV, wb_ev[0])
            rec.wait("tensor", sS, wb_ev[1])
            for c in range(ND):
                pg = pgA if c < 8 else pgB
                co = 128 * (c % 8)
                fn = lambda e, cc=c, pq=pg, cco=co: e.transpose(pq[:, cco:cco + 128], h_sb[:, 128 * cc:128 * (cc + 1)], ident[:])
                if c in (7, 15):
                    rec.inc("tensor", sPE, fn)
                    wb_tr[0 if c == 7 else 1] = sPE.n
                else:
                    rec.em("tensor", fn)
            rec.wait("vector", sPE, wb_tr[0])
            rec.wait("vector", sSy, wb_stage_dma)
            wb_ev[0] = rec.inc("vector", sV, lambda e: e.tensor_copy(stage[:, 2048:3072], pgA[:]))
            rec.wait("scalar", sPE, wb_tr[1])
            rec.wait("scalar", sSy, wb_stage_dma)
            wb_ev[1] = rec.inc("scalar", sS, lambda e: e.copy(stage[:, 3072:4096], pgB[:]))
            rec.wait("sync", sV, wb_ev[0])
            rec.wait("sync", sS, wb_ev[1])
            def wbspill(e, kk=k):
                src_ = stage[:, 2048:4096].rearrange("p (r c) -> p r c", c=128)
                dst = wtT_bf[:, 128 * kk:128 * (kk + 1)].rearrange("(r p) c -> p r c", p=128)
                return e.dma_start(out=dst, in_=src_)
            rec.inc("sync", sSy, wbspill, 16)
            wb_stage_dma = sSy.n
            if k + 2 < ND:
                rec.wait("gpsimd", sS, wb_scale_done)
                rec.inc("gpsimd", sWbL[k % 2], lambda e, kk=k + 2: e.dma_start(
                    out=x_sb[:, (2 + kk % 2) * D:(2 + kk % 2) * D + D],
                    in_=wf_h[128 * kk:128 * (kk + 1), :]), 16)
                wb_ld[k % 2] = sWbL[k % 2].n
        wt_built = sSy.n
        rec.inc("gpsimd", sGp, lambda e: e.dma_start(out=wo_bf, in_=wo_h, max_dma_last_dim=8192), 16)
        wo_cast = sGp.n

        t_pfA = {"v": 0, "s": 0}
        t_pfF = {"v": 0, "s": 0}
        t_pfB = {"v": 0, "s": 0}
        t_pfC = {"v": 0, "s": 0}
        t_pfD = {"v": 0, "s": 0}
        t_pfE = {"v": 0, "s": 0}
        t_pgB = {"v": wb_ev[0], "s": wb_ev[1], "s2": wb_ev[1]}
        t_pgA = {"v": wb_ev[0], "s": wb_ev[1]}
        t_sig_pe = 0
        t_exp_pe = 0
        t_delta_pe = 0
        t_attN_pe = 0
        t_amn_v = 0
        t_rcol_s = 0
        mb_upd_gate = [0] * H
        seg_pe_end = 0
        seg_v_end = 0
        seg_ld = 0
        expz = 0
        h2T_ready = [0, 0, 0, 0]
        op_prev_pe = 0
        attT_ready = [0] * H
        woq_dma = [0, 0, 0, 0]
        woq_pe = [0, 0]
        wtq_dma = [0] * 8
        wtq_pe = [0, 0]
        h_tr_prev = 0
        t_h2T_pe = 0
        out_dma = [0, 0, 0, 0]
        x_ld = [0, 0, 0, 0]
        pfA_banks = (pf[0], pf[5])
        pfB_, pfC_, pfD_, pfE_ = pf[1], pf[2], pf[3], pf[4]

        def emit_seg_loads(seg):
            rec.wait("sync", sPE, seg_pe_end)
            rec.wait("sync", sV, seg_v_end)
            def ldq(e, sg=seg):
                src_ = qT_sp[sg].rearrange("(h p) s -> p h s", p=128)
                dst = BIGA[:, 0:8192].rearrange("p (h s) -> p h s", s=512)
                return e.dma_start(out=dst, in_=src_)
            def ldk(e, sg=seg):
                src_ = kT_sp[sg].rearrange("(h p) s -> p h s", p=128)
                dst = BIGA[:, 8192:16384].rearrange("p (h s) -> p h s", s=512)
                return e.dma_start(out=dst, in_=src_)
            def ldv(e, sg=seg):
                src_ = v_sp[sg].rearrange("(t p) (h v) -> p h t v", p=128, v=128)
                dst = BIG0[:, 0:16 * 520].rearrange("p (h t c) -> p h t c", t=4, c=130)[:, :, :, 0:128]
                return e.dma_start(out=dst, in_=src_)
            if seg == 0:
                rec.wait("sync", sGp, vext_init)
            rec.inc("sync", sSy, ldq, 16)
            rec.inc("sync", sSy, ldk, 16)
            rec.inc("sync", sSy, ldv, 16)
            return sSy.n

        def emit_b_seg_start(seg):
            nonlocal seg_ld, expz
            if seg == 0:
                seg_ld = seg0_ld_val[0]
            else:
                seg_ld = emit_seg_loads(seg)
            expz = sGp.n


        sig_marks = [None] * H
        sig_pend = [None]
        attT_pend = [None]

        def flush_attT_evac():
            if attT_pend[0] is None:
                return
            hh0, a_attT0 = attT_pend[0]
            attT_pend[0] = None
            pga = pgB if hh0 % 2 == 0 else pgA
            tra = t_pgB if hh0 % 2 == 0 else t_pgA
            rec.wait("scalar", sPE, a_attT0)
            rec.wait("scalar", sPE, op_prev_pe)
            rec.inc("scalar", sS, lambda e, hh=hh0, pg_=pga: e.copy(BIG0[:, 8320 + 512 * hh:8320 + 512 * hh + 512],
                                                                    pg_[:, 512:1024]))
            tra["s"] = sS.n
            attT_ready[hh0] = sS.n

        def emit_b_sigma(seg, h):
            qh = BIGA[:, 512 * h:512 * h + 512]
            kh = BIGA[:, 8192 + 512 * h:8192 + 512 * h + 512]
            Mf = M_f32[:, 130 * h:130 * h + 130]
            Mb = M_bf16[:, 130 * (h % 2):130 * (h % 2) + 130]
            sq = sigqT[:, 512 * (h % 2):512 * (h % 2) + 512]
            sk = sigkT[:, 512 * (h % 2):512 * (h % 2) + 512]
            rec.wait("scalar", sPE, t_sig_pe)
            rec.wait("scalar", sV, mb_upd_gate[h])
            rec.inc("scalar", sS, lambda e, mb=Mb, mf=Mf: e.copy(mb[:], mf[:]))
            mb_ready = sS.n
            rec.wait("vector", sSy, seg_ld)
            rec.wait("vector", sPE, t_sig_pe)
            if sig_pend[0] is not None:
                rec.wait("vector", sS, sig_pend[0][5])
            rec.inc("vector", sV, lambda e, q=qh: e.tensor_scalar(sigtmp[:], q, 0.0, None, op0=ALU.min))
            rec.inc("vector", sV, lambda e, k2=kh: e.tensor_scalar(sigtmp3[:], k2, 0.0, None, op0=ALU.min))
            rec.wait("scalar", sV, sV.n)
            rec.em("scalar", lambda e: e.activation(sigtmp2[:], sigtmp[:], AF.Exp))
            rec.inc("scalar", sS, lambda e: e.activation(sigtmp[:], sigtmp3[:], AF.Exp))
            sig_pend[0] = (h, qh, kh, sq, sk, sS.n, mb_ready)

        def emit_b_sigma_stt():
            if sig_pend[0] is None:
                return
            h, qh, kh, sq, sk, s_exp, mb_ready = sig_pend[0]
            sig_pend[0] = None
            rec.wait("vector", sS, s_exp)
            rec.inc("vector", sV, lambda e, q=qh, d=sq: e.scalar_tensor_tensor(d, q, 0.0, sigtmp2[:], op0=ALU.max, op1=ALU.add))
            v_sigq = sV.n
            rec.inc("vector", sV, lambda e, k2=kh, d=sk: e.scalar_tensor_tensor(d, k2, 0.0, sigtmp[:], op0=ALU.max, op1=ALU.add))
            v_sigk = sV.n
            sig_marks[h] = (v_sigq, v_sigk, mb_ready)

        def emit_b_head(seg, h):
            nonlocal t_sig_pe, t_exp_pe, t_delta_pe, t_attN_pe, t_amn_v, t_rcol_s
            if True:
                qh = BIGA[:, 512 * h:512 * h + 512]
                kh = BIGA[:, 8192 + 512 * h:8192 + 512 * h + 512]
                Mf = M_f32[:, 130 * h:130 * h + 130]
                Mb = M_bf16[:, 130 * (h % 2):130 * (h % 2) + 130]
                if h + 1 < H:
                    emit_b_sigma(seg, h + 1)
                v_sigq, v_sigk, mb_ready = sig_marks[h]

                a_sc = [0] * 4
                e_done = [0] * 4
                mask_done = [0] * 4
                rec.wait("tensor", sSy, seg_ld)
                for i in range(4):
                    bank = pfA_banks[i % 2]
                    tr = t_pfA if i % 2 == 0 else t_pfF
                    if i < 2:
                        rec.wait("tensor", sV, tr["v"])
                        rec.wait("tensor", sS, tr["s"])
                    else:
                        rec.wait("tensor", sS, e_done[i - 2])
                    a_sc[i] = rec.inc("tensor", sPE,
                                      lambda e, b=bank, ii=i, k2=kh, q=qh:
                                      e.matmul(b[:, 128 * ii:512], k2[:, 128 * ii:128 * ii + 128], q[:, 128 * ii:512],
                                               start=True, stop=True))
                    rec.wait("scalar", sPE, a_sc[i])
                    if i < 3:
                        rec.em("scalar", lambda e, b=bank, ii=i:
                               e.activation(expS[:, TRI[ii] + 128:TRI[ii] + 512 - 128 * ii],
                                            b[:, 128 * (ii + 1):512], AF.Exp, scale=ISCALE))
                    rec.inc("scalar", sS, lambda e, b=bank, ii=i:
                            e.activation(dtmp[:, 128 * ii:128 * ii + 128],
                                         b[:, 128 * ii:128 * ii + 128], AF.Exp, scale=ISCALE))
                    e_done[i] = sS.n
                    rec.wait("vector", sS, e_done[i])
                    if i == 0:
                        rec.wait("vector", sPE, t_exp_pe)
                    mask_done[i] = rec.inc("vector", sV, lambda e, ii=i:
                                           e.tensor_tensor(expS[:, TRI[ii]:TRI[ii] + 128],
                                                           dtmp[:, 128 * ii:128 * ii + 128], trimask[:], op=ALU.mult))

                flush_attT_evac()

                rec.wait("tensor", sV, v_sigk)
                rec.wait("tensor", sS, mb_ready)
                rec.wait("tensor", sS, e_done[2])
                a_skm01 = None
                for i in range(4):
                    bank = pfA_banks[i % 2]
                    if i == 3:
                        rec.wait("tensor", sS, e_done[3])
                    cols = 130 * (i // 2)
                    fn = (lambda e, b=bank, ii=i, c=cols, mb=Mb, hp=h % 2:
                          e.matmul(b[:, c:c + 130], sigkT[:, 512 * hp + 128 * ii:512 * hp + 128 * ii + 128], mb[:],
                                   start=True, stop=True))
                    if i in (1, 3):
                        val = rec.inc("tensor", sPE, fn)
                        if i == 1:
                            a_skm01 = val
                        else:
                            a_skm23 = val
                    else:
                        rec.em("tensor", fn)

                rec.wait("tensor", sV, v_sigk)
                rec.wait("tensor", sS, t_pgB["s2"])
                rec.wait("tensor", sS, t_pgB["s"])
                for i in range(4):
                    fn = lambda e, ii=i, hp=h % 2: e.transpose(pgB[:, 128 * ii:128 * ii + 128],
                                                     sigkT[:, 512 * hp + 128 * ii:512 * hp + 128 * ii + 128], ident[:])
                    if i == 3:
                        a_trk = rec.inc("tensor", sPE, fn)
                    else:
                        rec.em("tensor", fn)
                rec.wait("scalar", sPE, a_trk)
                rec.inc("scalar", sS, lambda e: e.copy(sigkN[:], pgB[:, 0:512]))
                t_pgB["s2"] = sS.n
                sigkN_ev_s = sS.n

                rec.wait("vector", sPE, a_skm01)
                rec.wait("vector", sPE, t_delta_pe)
                for i in range(4):
                    if i == 2:
                        rec.wait("vector", sPE, a_skm23)
                    bank = pfA_banks[i % 2]
                    zc = 130 * (i // 2) + 129
                    fn = lambda e, b=bank, c=zc, ii=i: e.reciprocal(rcol[:, ii:ii + 1], b[:, c:c + 1])
                    if i == 3:
                        rec.inc("vector", sV, fn)
                    else:
                        rec.em("vector", fn)
                rec.wait("vector", sV, sV.n)
                for i in range(4):
                    bank = pfA_banks[i % 2]
                    vc = (4 * h + i) * 130
                    rec.inc("vector", sV, lambda e, b=bank, c=130 * (i // 2), ii=i, v=vc:
                            e.scalar_tensor_tensor(delta[:, 130 * ii:130 * ii + 128], b[:, c:c + 128],
                                                   rcol[:, ii:ii + 1], BIG0[:, v:v + 128],
                                                   op0=ALU.mult, op1=ALU.add))
                delta_done = sV.n
                t_pfA["v"] = delta_done
                t_pfF["v"] = delta_done
                t_pfA["s"] = e_done[2]
                t_pfF["s"] = e_done[3]

                rec.wait("tensor", sV, delta_done)
                rec.wait("tensor", sS, sigkN_ev_s)
                for i in range(4):
                    fn = (lambda e, ii=i: e.matmul(pfA_banks[1][:, 260:390], sigkN[:, 128 * ii:128 * ii + 128],
                                                   delta[:, 130 * ii:130 * ii + 130],
                                                   start=(ii == 0), stop=(ii == 3)))
                    if i == 3:
                        a_upd = rec.inc("tensor", sPE, fn)
                    else:
                        rec.em("tensor", fn)
                rec.wait("vector", sPE, a_upd)
                rec.inc("vector", sV, lambda e, mf=Mf: e.tensor_tensor(mf[:, 0:130], mf[:, 0:130], pfA_banks[1][:, 260:390], op=ALU.add))
                m_upd = sV.n
                t_pfF["v"] = max(t_pfF["v"], m_upd)

                rec.wait("tensor", sV, v_sigq)
                rec.wait("tensor", sV, t_pfB["v"])
                rec.wait("tensor", sS, t_pfB["s"])
                for j in range(4):
                    bank = pfB_ if j < 2 else pfC_
                    if j == 2:
                        rec.wait("tensor", sV, t_pfC["v"])
                        rec.wait("tensor", sS, t_pfC["s"])
                    cols = 130 * (j % 2)
                    fn = (lambda e, b=bank, jj=j, c=cols, mb=Mb, hp=h % 2:
                          e.matmul(b[:, c:c + 130], sigqT[:, 512 * hp + 128 * jj:512 * hp + 128 * jj + 128], mb[:],
                                   start=True, stop=True))
                    if j == 1:
                        a_am01 = rec.inc("tensor", sPE, fn)
                    elif j == 3:
                        a_am23 = rec.inc("tensor", sPE, fn)
                    else:
                        rec.em("tensor", fn)

                rec.wait("tensor", sV, mask_done[3])
                for j in range(4):
                    bank = pfD_ if j < 2 else pfE_
                    if j == 0:
                        rec.wait("tensor", sV, t_pfD["v"])
                        rec.wait("tensor", sS, t_pfD["s"])
                    if j == 2:
                        rec.wait("tensor", sV, t_pfE["v"])
                        rec.wait("tensor", sS, t_pfE["s"])
                    cols = 130 * (j % 2)
                    for i in range(j + 1):
                        vc = (4 * h + i) * 130
                        fn = (lambda e, b=bank, jj=j, ii=i, c=cols, v=vc:
                              e.matmul(b[:, c:c + 130],
                                       expS[:, TRI[ii] + 128 * (jj - ii):TRI[ii] + 128 * (jj - ii) + 128],
                                       BIG0[:, v:v + 130],
                                       start=(ii == 0), stop=(ii == jj)))
                        if j == 1 and i == 1:
                            a_ad01 = rec.inc("tensor", sPE, fn)
                        elif j == 3 and i == 3:
                            a_ad23 = rec.inc("tensor", sPE, fn)
                        else:
                            rec.em("tensor", fn)
                t_exp_pe = a_ad23
                t_sig_pe = a_am23
                t_delta_pe = a_upd

                rec.wait("vector", sPE, a_am01)
                for j in range(4):
                    if j == 2:
                        rec.wait("vector", sPE, a_am23)
                    bank = pfB_ if j < 2 else pfC_
                    zc = 130 * (j % 2) + 128
                    rec.inc("vector", sV, lambda e, b=bank, c=zc, jj=j: e.reciprocal(rcol[:, 4 + jj:5 + jj], b[:, c:c + 1]))
                rec.wait("vector", sPE, a_ad01)
                for j in range(4):
                    if j == 2:
                        rec.wait("vector", sPE, a_ad23)
                    bank = pfD_ if j < 2 else pfE_
                    zc = 130 * (j % 2) + 128
                    rec.inc("vector", sV, lambda e, b=bank, c=zc, jj=j: e.reciprocal(rcol[:, 8 + jj:9 + jj], b[:, c:c + 1]))
                recips = sV.n


                mb_upd_gate[h] = m_upd

                rec.wait("vector", sV, sV.n)
                rec.wait("vector", sPE, t_attN_pe)
                for j in range(4):
                    bad = pfD_ if j < 2 else pfE_
                    bam = pfB_ if j < 2 else pfC_
                    cols = 130 * (j % 2)
                    rec.em("vector", lambda e, b=bad, c=cols, jj=j:
                           e.tensor_scalar(adn[:, 128 * jj:128 * jj + 128], b[:, c:c + 128],
                                           rcol[:, 8 + jj:9 + jj], None, op0=ALU.mult))
                    rec.em("vector", lambda e, b=bam, c=cols, jj=j:
                           e.scalar_tensor_tensor(amn[:, 128 * jj:128 * jj + 128], b[:, c:c + 128],
                                                  rcol[:, 4 + jj:5 + jj], adn[:, 128 * jj:128 * jj + 128],
                                                  op0=ALU.mult, op1=ALU.subtract))
                    rec.em("vector", lambda e, jj=j, hh=h: e.tensor_tensor(amn[:, 128 * jj:128 * jj + 128],
                                                                           amn[:, 128 * jj:128 * jj + 128],
                                                                           betab[:, 128 * hh:128 * hh + 128], op=ALU.mult))
                    fn = (lambda e, jj=j: e.tensor_tensor(attN[:, 128 * jj:128 * jj + 128],
                                                          adn[:, 128 * jj:128 * jj + 128],
                                                          amn[:, 128 * jj:128 * jj + 128], op=ALU.add))
                    if j == 3:
                        comb_done = rec.inc("vector", sV, fn)
                    else:
                        rec.em("vector", fn)
                t_amn_v = comb_done
                emit_b_sigma_stt()
                t_rcol_s = comb_done
                t_pfB["s"] = 0
                t_pfC["s"] = 0
                t_pfD["s"] = 0
                t_pfE["s"] = 0
                t_pfB["v"] = comb_done
                t_pfC["v"] = comb_done
                t_pfD["v"] = comb_done
                t_pfE["v"] = comb_done

                pga = pgB if h % 2 == 0 else pgA
                tra = t_pgB if h % 2 == 0 else t_pgA
                rec.wait("tensor", sV, comb_done)
                if h % 2 == 0:
                    rec.wait("tensor", sS, sigkN_ev_s)
                rec.wait("tensor", sS, tra["s"])
                for j in range(4):
                    fn = lambda e, jj=j, pg_=pga: e.transpose(pg_[:, 512 + 128 * jj:512 + 128 * jj + 128],
                                                              attN[:, 128 * jj:128 * jj + 128], ident[:])
                    if j == 3:
                        a_attT = rec.inc("tensor", sPE, fn)
                    else:
                        rec.em("tensor", fn)
                t_attN_pe = a_attT
                attT_pend[0] = (h, a_attT)



        pending = []

        def drain_pending(k):
            for _ in range(k):
                if pending:
                    pending.pop(0)()

        def emit_c_op(seg):
            nonlocal op_prev_pe, h_tr_prev, t_h2T_pe
            if True:
                def ldwoq(e, cc=0):
                    src_ = wo_bf[:, 512 * cc:512 * (cc + 1)].rearrange("(b p) c -> p b c", p=128)
                    dst = BIGA[:, woq_off[cc % 2]:woq_off[cc % 2] + 8192].rearrange("p (b c) -> p b c", c=512)
                    return e.dma_start(out=dst, in_=src_)
                for c in range(2):
                    rec.wait("sync", sPE, woq_pe[c % 2])
                    rec.wait("sync", sGp, wo_cast)
                    woq_dma[c] = rec.inc("sync", sSy, lambda e, cc=c: ldwoq(e, cc), 16)

                attT_view = BIG0[:, 8320:8320 + 8192].rearrange("p (h x b) -> p h x b", h=16, b=16)
                opg = 0
                h2T_ready[:] = [0, 0, 0, 0]
                for c in range(4):
                    for r in range(4):
                        if c == 0:
                            rec.wait("sync", sOut, out_dma[r])
                            if seg == 0 and r >= 2:
                                rec.wait("sync", sS, wb_scale_done)
                            gr = 4 * seg + r
                            rec.inc("sync", sSy, lambda e, g=gr, sl=r:
                                    e.dma_start(out=x_sb[:, sl * D:sl * D + D], in_=x_h[128 * g:128 * (g + 1), :]), 16)
                            x_ld[r] = sSy.n
                        opbanks = (pfD_, pfE_, pfB_, pfC_)
                        optrk = (t_pfD, t_pfE, t_pfB, t_pfC)
                        bank = opbanks[opg % 4]
                        tb = optrk[opg % 4]
                        rec.wait("tensor", sSy, woq_dma[c])
                        rec.wait("tensor", sS, attT_ready[4 * r + 3])
                        rec.wait("tensor", sV, tb["v"])
                        rec.wait("tensor", sS, tb["s"])
                        for b in range(16):
                            fn = (lambda e, bb=b, rr=r, cc=c, bk=bank:
                                  e.matmul(bk[:], attT_view[:, 4 * rr:4 * rr + 4, :, bb:bb + 1],
                                           BIGA[:, woq_off[cc % 2] + 512 * bb:woq_off[cc % 2] + 512 * bb + 512],
                                           start=(bb == 0), stop=(bb == 15)))
                            if b == 15:
                                a_op = rec.inc("tensor", sPE, fn)
                            else:
                                rec.em("tensor", fn)
                        rec.wait("vector", sPE, a_op)
                        rec.wait("vector", sSy, x_ld[r])
                        rec.inc("vector", sV, lambda e, rr=r, cc=c, bk=bank:
                                e.tensor_tensor(x_sb[:, rr * D + 512 * cc:rr * D + 512 * cc + 512],
                                                bk[:], x_sb[:, rr * D + 512 * cc:rr * D + 512 * cc + 512],
                                                op=ALU.add))
                        optrk[opg % 4]["v"] = sV.n
                        opg += 1
                        if c == 3:
                            x2_done = sV.n
                            ln_done = ln_tile(x_sb[:, r * D:r * D + D], h_sb[:], sV, x2_done,
                                              extra_scalar_waits=((sPE, h_tr_prev),))
                            for w in range(2):
                                rec.wait("tensor", sS, ln_done)
                                rec.wait("tensor", sV, t_pgA["v"])
                                rec.wait("tensor", sS, t_pgA["s"])
                                for kk in range(8):
                                    k2 = 8 * w + kk
                                    fn = lambda e, kq=k2, co=128 * kk: e.transpose(pgA[:, co:co + 128],
                                                                                   h_sb[:, 128 * kq:128 * kq + 128], ident[:])
                                    if kk == 7:
                                        a_h2 = rec.inc("tensor", sPE, fn)
                                    else:
                                        rec.em("tensor", fn)
                                rec.wait("vector", sPE, a_h2)
                                rec.wait("vector", sPE, t_h2T_pe)
                                rec.inc("vector", sV, lambda e, rr=r, ww=w:
                                        e.tensor_copy(BIG0[:, h2T_off + 2048 * rr + 1024 * ww:h2T_off + 2048 * rr + 1024 * ww + 1024],
                                                      pgA[:]))
                                t_pgA["v"] = sV.n
                            h_tr_prev = sPE.n
                            h2T_ready[r] = t_pgA["s"]
                    woq_pe[c % 2] = sPE.n
                    if c + 2 < 4:
                        rec.wait("sync", sPE, woq_pe[c % 2])
                        woq_dma[c + 2] = rec.inc("sync", sSy, lambda e, cc=c + 2: ldwoq(e, cc), 16)
                op_prev_pe = sPE.n


        def queue_c_ffn(seg):
            h2T_rd = list(h2T_ready)
            def ldwtq(e, qq=0):
                src_ = wtT_bf[:, 256 * qq:256 * (qq + 1)].rearrange("(k p) c -> p k c", p=128)
                if qq % 2 == 0:
                    dst = BIG0[:, wtq_off:wtq_off + 4096].rearrange("p (k c) -> p k c", c=256)
                else:
                    dst = stage[:, 0:4096].rearrange("p (k c) -> p k c", c=256)
                return e.dma_start(out=dst, in_=src_)
            for q in range(2):
                rec.wait("sync", sPE, wtq_pe[q % 2])
                rec.wait("sync", sSy, wt_built)
                wtq_dma[q] = rec.inc("sync", sSy, lambda e, qq=q: ldwtq(e, qq), 16)
            out_last = [0, 0, 0, 0]
            ffg = [0]

            def emit_ffn_q(q, rlo, rhi):
                if q == 0 and rlo == 0:
                    rec.wait("tensor", sGp, bias_ld)
                for r in range(rlo, rhi):
                    bank = pfB_ if ffg[0] % 2 == 0 else pfC_
                    tb = t_pfB if ffg[0] % 2 == 0 else t_pfC
                    rec.wait("tensor", sSy, wtq_dma[q])
                    rec.wait("tensor", sV, tb["v"])
                    rec.wait("tensor", sS, max(tb["s"], h2T_rd[r]))
                    for k in range(ND):
                        if q % 2 == 0:
                            ws = BIG0[:, wtq_off + 256 * k:wtq_off + 256 * k + 256]
                        else:
                            ws = stage[:, 256 * k:256 * k + 256]
                        fn = (lambda e, kk2=k, rr=r, w2=ws, bk=bank:
                              e.matmul(bk[:, 0:256], BIG0[:, h2T_off + 2048 * rr + 128 * kk2:h2T_off + 2048 * rr + 128 * kk2 + 128],
                                       w2, start=(kk2 == 0), stop=False))
                        rec.em("tensor", fn)
                    a_ff = rec.inc("tensor", sPE, lambda e, qq=q, bk=bank:
                                   e.matmul(bk[:, 0:256], trimask[0:1, :], biasr[0:1, 256 * qq:256 * qq + 256],
                                            start=False, stop=True))
                    rec.wait("vector", sPE, a_ff)
                    xoff = r * D + 256 * q
                    rec.inc("vector", sV, lambda e, xo=xoff, bk=bank: e.tensor_tensor(x_sb[:, xo:xo + 256], bk[:, 0:256],
                                                                                      x_sb[:, xo:xo + 256], op=ALU.add))
                    if ffg[0] % 2 == 0:
                        t_pfB["v"] = sV.n
                    else:
                        t_pfC["v"] = sV.n
                    out_last[r] = sV.n
                    ffg[0] += 1
                if rhi == 4:
                    wtq_pe[q % 2] = sPE.n
                    if q + 2 < 8:
                        rec.wait("sync", sPE, wtq_pe[q % 2])
                        wtq_dma[q + 2] = rec.inc("sync", sSy, lambda e, qq=q + 2: ldwtq(e, qq), 16)
                    if q == 7:
                        nonlocal t_h2T_pe
                        t_h2T_pe = sPE.n

            def emit_outs():
                for r in range(4):
                    gr = 4 * seg + r
                    rec.wait("sync", sV, out_last[r])
                    rec.inc("sync", sOut, lambda e, g=gr, sl=r:
                            e.dma_start(out=out_h[128 * g:128 * (g + 1), :], in_=x_sb[:, sl * D:sl * D + D]), 16)
                    out_dma[r] = sOut.n

            for q in range(8):
                pending.append(lambda qq=q: emit_ffn_q(qq, 0, 4))
            pending.append(emit_outs)

        for seg in range(NSEG):
            emit_b_seg_start(seg)
            emit_b_sigma(seg, 0)
            emit_b_sigma_stt()
            for h in range(H):
                emit_b_head(seg, h)
                if h >= 4:
                    drain_pending(1)
            flush_attT_evac()
            seg_pe_end = sPE.n
            seg_v_end = sV.n
            drain_pending(17)
            emit_c_op(seg)
            queue_c_ffn(seg)
        drain_pending(17)

        rec.wait("sync", sOut, sOut.n)
        rec.run(nc)

    nc.finalize()
    return nc


def make_consts():
    ident = np.eye(128, dtype=ml_dtypes.bfloat16)
    tri = np.triu(np.ones((128, 128), dtype=np.float32))
    trimask = tri.astype(ml_dtypes.bfloat16)
    return ident, trimask


def make_inputs(L0_attn_x, L0_attn_Wq, L0_attn_Wk, L0_attn_Wv, L0_attn_Wo, L0_attn_betas,
                L0_ffn_weight, L0_ffn_scale_weight, L0_ffn_bias):
    x = np.ascontiguousarray(np.asarray(L0_attn_x, dtype=np.float32))
    ident, trimask = make_consts()
    betas = np.asarray(L0_attn_betas, dtype=np.float32).reshape(H, DK)
    betab = np.ascontiguousarray(np.broadcast_to(betas.reshape(1, D), (128, D))).astype(np.float32)
    bias = np.asarray(L0_ffn_bias, dtype=np.float32)
    biasb = np.ascontiguousarray(np.broadcast_to(bias.reshape(1, D), (128, D))).astype(np.float32)
    scale = np.asarray(L0_ffn_scale_weight, dtype=np.float32)
    fscale = np.ascontiguousarray(scale.reshape(ND, 128).T).astype(np.float32)
    base = {"wq": np.ascontiguousarray(np.asarray(L0_attn_Wq, np.float32)),
            "wk": np.ascontiguousarray(np.asarray(L0_attn_Wk, np.float32)),
            "wv": np.ascontiguousarray(np.asarray(L0_attn_Wv, np.float32)),
            "wo": np.ascontiguousarray(np.asarray(L0_attn_Wo, np.float32)),
            "wf": np.ascontiguousarray(np.asarray(L0_ffn_weight, np.float32)),
            "ident": ident, "trimask": trimask, "betab": betab, "biasb": biasb,
            "fscale": fscale}
    return [dict(base, x=np.ascontiguousarray(x[b])) for b in range(8)]


def kernel(**inputs):
    in_maps = make_inputs(**inputs)
    nc = build_program()
    res = run_bass_kernel_spmd(nc, in_maps, core_ids=list(range(8)))
    return np.stack([r["out"] for r in res.results], axis=0)


# version 23 (final)
# speedup vs baseline: 1.1949x; 1.1949x over previous
import os
import numpy as np
import ml_dtypes
import concourse.bass as bass
import concourse.mybir as mybir
from concourse.bass_utils import run_bass_kernel_spmd
from concourse.alu_op_type import AluOpType as ALU

F32 = mybir.dt.float32
BF16 = mybir.dt.bfloat16
AF = mybir.ActivationFunctionType

D = 2048
S = 2048
H = 16
DK = 128
SEG = 512
NSEG = S // SEG
NT = S // 128
ND = D // 128
EPS = 1e-5
ISCALE = 1.0 / float(np.sqrt(np.float32(DK)))

PHASES = os.environ.get("MK_PHASES", "ABC")
DEV = os.environ.get("MK_DEV", "0") == "1"


class Rec:
    engines = ("tensor", "vector", "scalar", "sync", "gpsimd")

    def __init__(self):
        self.q = {e: [] for e in self.engines}

    def em(self, eng, fn):
        self.q[eng].append(fn)

    def wait(self, eng, sem, val):
        if val <= 0:
            return
        assert val <= sem.n, f"wait target {val} > recorded {sem.n} on {sem.name}"
        self.q[eng].append(lambda e, s=sem.h, v=val: e.wait_ge(s, v))

    def inc(self, eng, sem, fn, k=1):
        self.q[eng].append(lambda e, f=fn, s=sem.h, kk=k: f(e).then_inc(s, kk))
        sem.n += k
        return sem.n

    def run(self, nc):
        with nc.Block() as blk:
            for e in self.engines:
                if self.q[e]:
                    getattr(blk, e)(lambda eng, fns=tuple(self.q[e]): [f(eng) for f in fns])
        self.q = {e: [] for e in self.engines}


class Sem:
    def __init__(self, nc, name):
        self.h = nc.alloc_semaphore(name)
        self.n = 0
        self.name = name


def build_program(phases=PHASES, dev=DEV):
    nc = bass.Bass("TRN2", target_bir_lowering=False, debug=False)

    x_h = nc.dram_tensor("x", [S, D], F32, kind="ExternalInput").ap()
    wq_h = nc.dram_tensor("wq", [D, D], F32, kind="ExternalInput").ap()
    wk_h = nc.dram_tensor("wk", [D, D], F32, kind="ExternalInput").ap()
    wv_h = nc.dram_tensor("wv", [D, D], F32, kind="ExternalInput").ap()
    wo_h = nc.dram_tensor("wo", [D, D], F32, kind="ExternalInput").ap()
    wf_h = nc.dram_tensor("wf", [D, D], F32, kind="ExternalInput").ap()
    ident_h = nc.dram_tensor("ident", [128, 128], BF16, kind="ExternalInput").ap()
    trimask_h = nc.dram_tensor("trimask", [128, 128], BF16, kind="ExternalInput").ap()
    betab_h = nc.dram_tensor("betab", [128, D], F32, kind="ExternalInput").ap()
    biasb_h = nc.dram_tensor("biasb", [128, D], F32, kind="ExternalInput").ap()
    fscale_h = nc.dram_tensor("fscale", [128, ND], F32, kind="ExternalInput").ap()

    spill_kind = "ExternalOutput" if (dev and "C" not in phases) else "Internal"
    qT_sp = nc.dram_tensor("qT_sp", [NSEG, D, SEG], BF16, kind=spill_kind).ap()
    kT_sp = nc.dram_tensor("kT_sp", [NSEG, D, SEG], BF16, kind=spill_kind).ap()
    v_sp = nc.dram_tensor("v_sp", [NSEG, SEG, D], BF16, kind=spill_kind).ap()
    attT_kind = "ExternalOutput" if (dev and "C" not in phases and "B" in phases) else "Internal"
    attT_sp = nc.dram_tensor("attT_sp", [NSEG, D, SEG], BF16, kind=attT_kind).ap()
    out_h = None
    if "C" in phases:
        out_h = nc.dram_tensor("out", [S, D], F32, kind="ExternalOutput").ap()

    sb = nc.alloc_sbuf_tensor
    BIG0 = sb("BIG0", [128, 32768], BF16)
    BIGA = sb("BIGA", [128, 32768], BF16)
    x_sb = sb("x_sb", [128, 4 * D], F32)
    stage = sb("stage", [128, 8 * 512], BF16)
    betab = sb("betab_sb", [128, D], F32)
    fscale = sb("fscale_sb", [128, ND], F32)
    h_sb = sb("h_sb", [128, D], BF16)
    M_f32 = sb("M_f32", [128, H * 130], F32)
    M_bf16 = sb("M_bf16", [128, 2 * 130], BF16)
    sigqT = sb("sigqT", [128, 2 * SEG], BF16)
    sigkT = sb("sigkT", [128, 2 * SEG], BF16)
    sigkN = sb("sigkN", [128, SEG], BF16)
    attN = sb("attN", [128, SEG], BF16)
    delta = sb("delta", [128, 4 * 130], BF16)
    amn = sb("amn", [128, SEG], F32)
    adn = sb("adn", [128, SEG], F32)
    sigtmp = sb("sigtmp", [128, SEG], BF16)
    sigtmp2 = sb("sigtmp2", [128, SEG], BF16)
    sigtmp3 = sb("sigtmp3", [128, SEG], BF16)
    dtmp = sb("dtmp", [128, SEG], BF16)
    ident = sb("ident_sb", [128, 128], BF16)
    trimask = sb("trimask_sb", [128, 128], BF16)
    bnbuf = sb("bnbuf", [128, 24], F32)
    bnout = sb("bnout", [128, 2], F32)
    rstd = sb("rstd", [128, 1], F32)
    negmr = sb("negmr", [128, 1], F32)
    lntmp = sb("lntmp", [128, 1], F32)
    rcol = sb("rcol", [128, 12], F32)

    hT = BIG0
    TRI = (0, 512, 896, 1152)
    expS = BIG0[:, 28800:28800 + 1280]
    biasr = BIG0[:, 30080:30080 + 2048]
    wT = BIG0
    Wo_sb = BIGA
    qT_off, kT_off = 0, 8192
    vext_off = 0
    attT_off = 16 * 520

    pf = [nc.alloc_psum_tensor(f"pf{i}", [128, 512], F32) for i in range(6)]
    pgA = nc.alloc_psum_tensor("pgA", [128, 1024], BF16)
    pgB = nc.alloc_psum_tensor("pgB", [128, 1024], BF16)

    sPE = Sem(nc, "sPE")
    sV = Sem(nc, "sV")
    sS = Sem(nc, "sS")
    sSy = Sem(nc, "sSy")
    sGp = Sem(nc, "sGp")
    sOut = Sem(nc, "sOut")
    sWbL = [Sem(nc, "sWbL0"), Sem(nc, "sWbL1")]

    rec = Rec()

    for dst, src in ((ident, ident_h), (trimask, trimask_h), (betab, betab_h),
                     (fscale, fscale_h)):
        rec.inc("sync", sSy, lambda e, d=dst, s=src: e.dma_start(out=d[:], in_=s), 16)
    consts_ld = sSy.n
    bias_pend = [True]
    rec.wait("scalar", sSy, consts_ld)
    rec.inc("scalar", sS, lambda e: e.activation(betab[:], betab[:], AF.Exp, scale=-1.0))
    rec.wait("vector", sS, sS.n)
    rec.inc("vector", sV, lambda e: e.tensor_scalar_add(betab[:], betab[:], 1.0))
    rec.wait("vector", sV, sV.n)
    rec.inc("vector", sV, lambda e: e.reciprocal(betab[:], betab[:]))
    beta_done = sV.n
    rec.inc("vector", sV, lambda e: e.memset(M_f32[:], 0.0))
    rec.wait("vector", sV, sV.n)
    for h in range(H):
        rec.inc("vector", sV, lambda e, c=130 * h + 128: e.memset(M_f32[:, c:c + 1], 1.0 / 128.0))
        rec.inc("vector", sV, lambda e, c=130 * h + 129: e.memset(M_f32[:, c:c + 1], -1.0 / 128.0))
    rec.wait("scalar", sV, sV.n)
    m_ready = sS.n

    ln_prev_norm = [0]

    def ln_tile(src_ap, dst_ap, ready_sem, ready_val, extra_scalar_waits=()):
        rec.wait("vector", ready_sem, ready_val)
        rec.wait("vector", sS, ln_prev_norm[0])
        for c in range(4):
            fn = lambda e, cc=c, s=src_ap: e.bn_stats(bnbuf[:, 6 * cc:6 * cc + 6], s[:, 512 * cc:512 * cc + 512])
            if c == 3:
                rec.inc("vector", sV, fn)
            else:
                rec.em("vector", fn)
        rec.wait("vector", sV, sV.n)
        rec.inc("vector", sV, lambda e: e.bn_aggr(bnout[:], bnbuf[:]))
        rec.wait("vector", sV, sV.n)
        rec.inc("vector", sV, lambda e: e.tensor_scalar_add(lntmp[:], bnout[:, 1:2], EPS))
        rec.wait("scalar", sV, sV.n)
        rec.em("scalar", lambda e: e.activation(negmr[:], lntmp[:], AF.Ln))
        rec.inc("scalar", sS, lambda e: e.activation(rstd[:], negmr[:], AF.Exp, scale=-0.5))
        rec.wait("vector", sS, sS.n)
        rec.inc("vector", sV, lambda e: e.scalar_tensor_tensor(negmr[:], bnout[:, 0:1], -1.0, rstd[:], op0=ALU.mult, op1=ALU.mult))
        rec.wait("scalar", sV, sV.n)
        for sem, val in extra_scalar_waits:
            rec.wait("scalar", sem, val)
        rec.inc("scalar", sS, lambda e, s=src_ap, d=dst_ap: e.activation(d, s, AF.Identity, bias=negmr[:], scale=rstd[:]))
        ln_prev_norm[0] = sS.n
        return sS.n

    slot_last_dma = [0] * 8
    spill_ctr = [0]

    def spill_psum(bank_ap, dest_ap, pe_val, eng):
        g = spill_ctr[0]
        spill_ctr[0] += 1
        slot = g % 8
        st_ap = stage[:, 512 * slot:512 * slot + 512]
        sem = sS if eng == "scalar" else sV
        rec.wait(eng, sPE, pe_val)
        rec.wait(eng, sSy, slot_last_dma[slot])
        if eng == "scalar":
            rec.inc("scalar", sS, lambda e, b=bank_ap, st=st_ap: e.copy(st, b))
        else:
            rec.inc("vector", sV, lambda e, b=bank_ap, st=st_ap: e.tensor_copy(st, b))
        ev_val = sem.n
        rec.wait("sync", sem, ev_val)
        rec.inc("sync", sSy, lambda e, d=dest_ap, st=st_ap: e.dma_start(out=d, in_=st), 16)
        slot_last_dma[slot] = sSy.n
        return ev_val

    if "A" in phases:
        def emit_w_half_load(w_ap, hf, buf, gate_val=0):
            rec.wait("gpsimd", sPE, gate_val)
            for b4 in range(4):
                def f(e, w=w_ap, b4=b4, hh=hf, bb=buf):
                    src = w[512 * b4:512 * (b4 + 1), 1024 * hh:1024 * (hh + 1)]
                    src = src.rearrange("(k p) c -> p k c", p=128)
                    dst = BIGA[:, 16384 * bb + 4096 * b4:16384 * bb + 4096 * (b4 + 1)]
                    dst = dst.rearrange("p (k c) -> p k c", c=1024)
                    return e.dma_start(out=dst, in_=src)
                rec.inc("gpsimd", sGp, f, 16)
            return sGp.n

        ln_scal_done = [0, 0]
        tr_done = [0, 0]
        ev_done = [0, 0]
        for i in range(NT):
            xa = x_sb[:, (i % 2) * D:(i % 2) * D + D]
            rec.wait("sync", sS, ln_scal_done[i % 2])
            if i == 0:
                rec.wait("sync", sV, beta_done)
            rec.inc("sync", sSy, lambda e, xs=xa, ii=i: e.dma_start(out=xs, in_=x_h[128 * ii:128 * (ii + 1), :]), 16)
            ln_scal_done[i % 2] = ln_tile(xa, h_sb[:], sSy, sSy.n, extra_scalar_waits=((sPE, tr_done[1]),))
            rec.wait("tensor", sS, ln_scal_done[i % 2])
            rec.wait("tensor", sV, ev_done[0])
            rec.wait("tensor", sS, ev_done[1])
            for k in range(ND):
                pg = pgA if k < 8 else pgB
                co = 128 * (k % 8)
                fn = lambda e, kk=k, p=pg, c=co: e.transpose(p[:, c:c + 128], h_sb[:, 128 * kk:128 * (kk + 1)], ident[:])
                if k in (7, 15):
                    rec.inc("tensor", sPE, fn)
                    tr_done[0 if k == 7 else 1] = sPE.n
                else:
                    rec.em("tensor", fn)
            def evA(e, ii=i):
                src = pgA[:].rearrange("p (k c) -> p k c", c=128)
                dst = hT[:].rearrange("p (k s) -> p k s", s=2048)[:, 0:8, 128 * ii:128 * ii + 128]
                return e.tensor_copy(dst, src)
            def evB(e, ii=i):
                src = pgB[:].rearrange("p (k c) -> p k c", c=128)
                dst = hT[:].rearrange("p (k s) -> p k s", s=2048)[:, 8:16, 128 * ii:128 * ii + 128]
                return e.copy(dst, src)
            rec.wait("vector", sPE, tr_done[0])
            ev_done[0] = rec.inc("vector", sV, evA)
            rec.wait("scalar", sPE, tr_done[1])
            ev_done[1] = rec.inc("scalar", sS, evB)
        hT_ready_v, hT_ready_s = ev_done

        load_done = [0] * 6
        mm_done_gate = [0] * 6
        bank_ev = [0, 0]
        gctr = [0]
        first_group = [True]

        def mm_group_prelude(g, w_target):
            bank = pf[g % 2]
            eng = "scalar" if g % 2 == 0 else "vector"
            sem = sS if g % 2 == 0 else sV
            rec.wait("tensor", sGp, w_target)
            if first_group[0]:
                rec.wait("tensor", sV, hT_ready_v)
                rec.wait("tensor", sS, hT_ready_s)
                first_group[0] = False
            rec.wait("tensor", sem, bank_ev[g % 2])
            return bank, eng

        def emit_qk(widx, w_ap, dest_sp):
            for hf in range(2):
                li = widx * 2 + hf
                load_done[li] = emit_w_half_load(w_ap, hf, hf, mm_done_gate[li])
                for m in range(8):
                    for c in range(NSEG):
                        g = gctr[0]; gctr[0] += 1
                        bank, eng = mm_group_prelude(g, load_done[li])
                        for k in range(ND):
                            fn = (lambda e, b=bank, hh=hf, mm=m, cc=c, kk=k:
                                  e.matmul(b[:],
                                           BIGA[:, 16384 * hh + 1024 * kk + 128 * mm:16384 * hh + 1024 * kk + 128 * mm + 128],
                                           hT[:, 2048 * kk + 512 * cc:2048 * kk + 512 * cc + 512],
                                           start=(kk == 0), stop=(kk == ND - 1)))
                            if k == ND - 1:
                                rec.inc("tensor", sPE, fn)
                            else:
                                rec.em("tensor", fn)
                        mrow = 128 * (8 * hf + m)
                        bank_ev[g % 2] = spill_psum(bank[:], dest_sp[c, mrow:mrow + 128, :], sPE.n, eng)
                nli = li + 2
                if nli < 6:
                    mm_done_gate[nli] = sPE.n

        def emit_v(w_ap, dest_sp):
            for hf in range(2):
                li = 4 + hf
                load_done[li] = emit_w_half_load(w_ap, hf, hf, mm_done_gate[li])
                for st_ in range(NT):
                    for c2 in range(2):
                        g = gctr[0]; gctr[0] += 1
                        bank, eng = mm_group_prelude(g, load_done[li])
                        for k in range(ND):
                            fn = (lambda e, b=bank, hh=hf, ss=st_, cc=c2, kk=k:
                                  e.matmul(b[:],
                                           hT[:, 2048 * kk + 128 * ss:2048 * kk + 128 * ss + 128],
                                           BIGA[:, 16384 * hh + 1024 * kk + 512 * cc:16384 * hh + 1024 * kk + 512 * cc + 512],
                                           start=(kk == 0), stop=(kk == ND - 1)))
                            if k == ND - 1:
                                rec.inc("tensor", sPE, fn)
                            else:
                                rec.em("tensor", fn)
                        seg, srow = st_ // 4, 128 * (st_ % 4)
                        dcol = 1024 * hf + 512 * c2
                        bank_ev[g % 2] = spill_psum(bank[:], dest_sp[seg, srow:srow + 128, dcol:dcol + 512], sPE.n, eng)

        emit_qk(0, wq_h, qT_sp)
        emit_qk(1, wk_h, kT_sp)
        emit_v(wv_h, v_sp)
        rec.wait("sync", sSy, sSy.n)
        rec.run(nc)

    if "B" in phases:
        rec.inc("gpsimd", sGp, lambda e: e.dma_start(out=biasr[0:1, :], in_=biasb_h[0:1, :]), 16)
        bias_ld = sGp.n
        h2T_off = 16512
        wtq_off = 24704
        woq_off = (16384, 24576)

        wo_bf = nc.dram_tensor("wo_bf", [D, D], BF16, kind="Internal").ap()
        wtT_bf = nc.dram_tensor("wtT_bf", [D, D], BF16, kind="Internal").ap()

        rec.inc("gpsimd", sGp, lambda e: e.memset(BIG0[:, 0:16 * 520], 0.0))
        rec.wait("gpsimd", sGp, sGp.n)
        for ht in range(16 * 4):
            rec.inc("gpsimd", sGp, lambda e, c=130 * ht + 128: e.memset(BIG0[:, c:c + 1], 1.0))
        rec.inc("gpsimd", sGp, lambda e: e.memset(delta[:], 0.0))
        rec.wait("gpsimd", sGp, sGp.n)
        for i in range(4):
            rec.inc("gpsimd", sGp, lambda e, c=130 * i + 128: e.memset(delta[:, c:c + 1], 1.0))
            rec.inc("gpsimd", sGp, lambda e, c=130 * i + 129: e.memset(delta[:, c:c + 1], -1.0))
        vext_init = sGp.n

        wb_scale_done = 0
        wb_tr = [0, 0]
        wb_ev = [0, 0]
        wb_stage_dma = [0, 0]
        wb_ld = [0, 0]
        rec.wait("sync", sGp, vext_init)

        def _ldq0(e):
            src_ = qT_sp[0].rearrange("(h p) s -> p h s", p=128)
            dst = BIGA[:, 0:8192].rearrange("p (h s) -> p h s", s=512)
            return e.dma_start(out=dst, in_=src_)

        def _ldk0(e):
            src_ = kT_sp[0].rearrange("(h p) s -> p h s", p=128)
            dst = BIGA[:, 8192:16384].rearrange("p (h s) -> p h s", s=512)
            return e.dma_start(out=dst, in_=src_)

        def _ldv0(e):
            src_ = v_sp[0].rearrange("(t p) (h v) -> p h t v", p=128, v=128)
            dst = BIG0[:, 0:16 * 520].rearrange("p (h t c) -> p h t c", t=4, c=130)[:, :, :, 0:128]
            return e.dma_start(out=dst, in_=src_)
        rec.inc("sync", sSy, _ldq0, 16)
        rec.inc("sync", sSy, _ldk0, 16)
        rec.inc("sync", sSy, _ldv0, 16)
        seg0_ld_val = [sSy.n]
        for k in range(2):
            rec.inc("gpsimd", sWbL[k % 2], lambda e, kk=k: e.dma_start(
                out=x_sb[:, (2 + kk % 2) * D:(2 + kk % 2) * D + D],
                in_=wf_h[128 * kk:128 * (kk + 1), :]), 16)
            wb_ld[k] = sWbL[k % 2].n
        for k in range(ND):
            rec.wait("scalar", sWbL[k % 2], wb_ld[k % 2])
            rec.wait("scalar", sPE, wb_tr[1])
            rec.inc("scalar", sS, lambda e, kk=k: e.activation(
                h_sb[:], x_sb[:, (2 + kk % 2) * D:(2 + kk % 2) * D + D], AF.Identity,
                scale=fscale[:, kk:kk + 1]))
            wb_scale_done = sS.n

            rec.wait("tensor", sS, wb_scale_done)
            rec.wait("tensor", sV, wb_ev[0])
            rec.wait("tensor", sS, wb_ev[1])
            for c in range(ND):
                pg = pgA if c < 8 else pgB
                co = 128 * (c % 8)
                fn = lambda e, cc=c, pq=pg, cco=co: e.transpose(pq[:, cco:cco + 128], h_sb[:, 128 * cc:128 * (cc + 1)], ident[:])
                if c in (7, 15):
                    rec.inc("tensor", sPE, fn)
                    wb_tr[0 if c == 7 else 1] = sPE.n
                else:
                    rec.em("tensor", fn)
            area = 2048 * (k % 2)
            rec.wait("vector", sPE, wb_tr[0])
            rec.wait("vector", sSy, wb_stage_dma[k % 2])
            wb_ev[0] = rec.inc("vector", sV, lambda e, ar=area: e.tensor_copy(stage[:, ar:ar + 1024], pgA[:]))
            rec.wait("scalar", sPE, wb_tr[1])
            rec.wait("scalar", sSy, wb_stage_dma[k % 2])
            wb_ev[1] = rec.inc("scalar", sS, lambda e, ar=area: e.copy(stage[:, ar + 1024:ar + 2048], pgB[:]))
            rec.wait("sync", sV, wb_ev[0])
            rec.wait("sync", sS, wb_ev[1])
            def wbspill(e, kk=k, ar=area):
                src_ = stage[:, ar:ar + 2048].rearrange("p (r c) -> p r c", c=128)
                dst = wtT_bf[:, 128 * kk:128 * (kk + 1)].rearrange("(r p) c -> p r c", p=128)
                return e.dma_start(out=dst, in_=src_)
            rec.inc("sync", sSy, wbspill, 16)
            wb_stage_dma[k % 2] = sSy.n
            if k + 2 < ND:
                rec.wait("gpsimd", sS, wb_scale_done)
                rec.inc("gpsimd", sWbL[k % 2], lambda e, kk=k + 2: e.dma_start(
                    out=x_sb[:, (2 + kk % 2) * D:(2 + kk % 2) * D + D],
                    in_=wf_h[128 * kk:128 * (kk + 1), :]), 16)
                wb_ld[k % 2] = sWbL[k % 2].n
        wt_built = sSy.n
        rec.inc("gpsimd", sGp, lambda e: e.dma_start(out=wo_bf, in_=wo_h, max_dma_last_dim=8192), 16)
        wo_cast = sGp.n

        t_pfA = {"v": 0, "s": 0}
        t_pfF = {"v": 0, "s": 0}
        t_pfB = {"v": 0, "s": 0}
        t_pfC = {"v": 0, "s": 0}
        t_pfD = {"v": 0, "s": 0}
        t_pfE = {"v": 0, "s": 0}
        t_pgB = {"v": wb_ev[0], "s": wb_ev[1], "s2": wb_ev[1]}
        t_pgA = {"v": wb_ev[0], "s": wb_ev[1]}
        t_sig_pe = 0
        t_exp_pe = 0
        t_delta_pe = 0
        t_attN_pe = 0
        t_amn_v = 0
        t_rcol_s = 0
        mb_upd_gate = [0] * H
        seg_pe_end = 0
        seg_v_end = 0
        seg_ld = 0
        expz = 0
        h2T_ready = [0, 0, 0, 0]
        op_prev_pe = 0
        attT_ready = [0] * H
        woq_dma = [0, 0, 0, 0]
        woq_pe = [0, 0]
        wtq_dma = [0] * 8
        wtq_pe = [0, 0]
        h_tr_prev = 0
        t_h2T_pe = 0
        out_dma = [0, 0, 0, 0]
        x_ld = [0, 0, 0, 0]
        pfA_banks = (pf[0], pf[5])
        pfB_, pfC_, pfD_, pfE_ = pf[1], pf[2], pf[3], pf[4]

        def emit_seg_loads(seg):
            rec.wait("sync", sPE, seg_pe_end)
            rec.wait("sync", sV, seg_v_end)
            def ldq(e, sg=seg):
                src_ = qT_sp[sg].rearrange("(h p) s -> p h s", p=128)
                dst = BIGA[:, 0:8192].rearrange("p (h s) -> p h s", s=512)
                return e.dma_start(out=dst, in_=src_)
            def ldk(e, sg=seg):
                src_ = kT_sp[sg].rearrange("(h p) s -> p h s", p=128)
                dst = BIGA[:, 8192:16384].rearrange("p (h s) -> p h s", s=512)
                return e.dma_start(out=dst, in_=src_)
            def ldv(e, sg=seg):
                src_ = v_sp[sg].rearrange("(t p) (h v) -> p h t v", p=128, v=128)
                dst = BIG0[:, 0:16 * 520].rearrange("p (h t c) -> p h t c", t=4, c=130)[:, :, :, 0:128]
                return e.dma_start(out=dst, in_=src_)
            if seg == 0:
                rec.wait("sync", sGp, vext_init)
            rec.inc("sync", sSy, ldq, 16)
            rec.inc("sync", sSy, ldk, 16)
            rec.inc("sync", sSy, ldv, 16)
            return sSy.n

        def emit_b_seg_start(seg):
            nonlocal seg_ld, expz
            if seg == 0:
                seg_ld = seg0_ld_val[0]
            else:
                seg_ld = emit_seg_loads(seg)
            expz = sGp.n


        sig_marks = [None] * H
        sig_pend = [None]
        attT_pend = [None]

        def flush_attT_evac():
            if attT_pend[0] is None:
                return
            hh0, a_attT0 = attT_pend[0]
            attT_pend[0] = None
            pga = pgB if hh0 % 2 == 0 else pgA
            tra = t_pgB if hh0 % 2 == 0 else t_pgA
            rec.wait("scalar", sPE, a_attT0)
            rec.wait("scalar", sPE, op_prev_pe)
            rec.inc("scalar", sS, lambda e, hh=hh0, pg_=pga: e.copy(BIG0[:, 8320 + 512 * hh:8320 + 512 * hh + 512],
                                                                    pg_[:, 512:1024]))
            tra["s"] = sS.n
            attT_ready[hh0] = sS.n

        def emit_b_sigma(seg, h):
            qh = BIGA[:, 512 * h:512 * h + 512]
            kh = BIGA[:, 8192 + 512 * h:8192 + 512 * h + 512]
            Mf = M_f32[:, 130 * h:130 * h + 130]
            Mb = M_bf16[:, 130 * (h % 2):130 * (h % 2) + 130]
            sq = sigqT[:, 512 * (h % 2):512 * (h % 2) + 512]
            sk = sigkT[:, 512 * (h % 2):512 * (h % 2) + 512]
            rec.wait("scalar", sPE, t_sig_pe)
            rec.wait("scalar", sV, mb_upd_gate[h])
            rec.inc("scalar", sS, lambda e, mb=Mb, mf=Mf: e.copy(mb[:], mf[:]))
            mb_ready = sS.n
            rec.wait("vector", sSy, seg_ld)
            rec.wait("vector", sPE, t_sig_pe)
            if sig_pend[0] is not None:
                rec.wait("vector", sS, sig_pend[0][5])
            rec.inc("vector", sV, lambda e, q=qh: e.tensor_scalar(sigtmp[:], q, 0.0, None, op0=ALU.min))
            rec.inc("vector", sV, lambda e, k2=kh: e.tensor_scalar(sigtmp3[:], k2, 0.0, None, op0=ALU.min))
            rec.wait("scalar", sV, sV.n)
            rec.em("scalar", lambda e: e.activation(sigtmp2[:], sigtmp[:], AF.Exp))
            rec.inc("scalar", sS, lambda e: e.activation(sigtmp[:], sigtmp3[:], AF.Exp))
            sig_pend[0] = (h, qh, kh, sq, sk, sS.n, mb_ready)

        def emit_b_sigma_stt():
            if sig_pend[0] is None:
                return
            h, qh, kh, sq, sk, s_exp, mb_ready = sig_pend[0]
            sig_pend[0] = None
            rec.wait("vector", sS, s_exp)
            rec.inc("vector", sV, lambda e, q=qh, d=sq: e.scalar_tensor_tensor(d, q, 0.0, sigtmp2[:], op0=ALU.max, op1=ALU.add))
            v_sigq = sV.n
            rec.inc("vector", sV, lambda e, k2=kh, d=sk: e.scalar_tensor_tensor(d, k2, 0.0, sigtmp[:], op0=ALU.max, op1=ALU.add))
            v_sigk = sV.n
            sig_marks[h] = (v_sigq, v_sigk, mb_ready)

        def emit_b_head(seg, h):
            nonlocal t_sig_pe, t_exp_pe, t_delta_pe, t_attN_pe, t_amn_v, t_rcol_s
            if True:
                qh = BIGA[:, 512 * h:512 * h + 512]
                kh = BIGA[:, 8192 + 512 * h:8192 + 512 * h + 512]
                Mf = M_f32[:, 130 * h:130 * h + 130]
                Mb = M_bf16[:, 130 * (h % 2):130 * (h % 2) + 130]
                if h + 1 < H:
                    emit_b_sigma(seg, h + 1)
                v_sigq, v_sigk, mb_ready = sig_marks[h]

                a_sc = [0] * 4
                e_done = [0] * 4
                mask_done = [0] * 4
                rec.wait("tensor", sSy, seg_ld)
                for i in range(4):
                    bank = pfA_banks[i % 2]
                    tr = t_pfA if i % 2 == 0 else t_pfF
                    if i < 2:
                        rec.wait("tensor", sV, tr["v"])
                        rec.wait("tensor", sS, tr["s"])
                    else:
                        rec.wait("tensor", sS, e_done[i - 2])
                    a_sc[i] = rec.inc("tensor", sPE,
                                      lambda e, b=bank, ii=i, k2=kh, q=qh:
                                      e.matmul(b[:, 128 * ii:512], k2[:, 128 * ii:128 * ii + 128], q[:, 128 * ii:512],
                                               start=True, stop=True))
                    rec.wait("scalar", sPE, a_sc[i])
                    if i < 3:
                        rec.em("scalar", lambda e, b=bank, ii=i:
                               e.activation(expS[:, TRI[ii] + 128:TRI[ii] + 512 - 128 * ii],
                                            b[:, 128 * (ii + 1):512], AF.Exp, scale=ISCALE))
                    rec.inc("scalar", sS, lambda e, b=bank, ii=i:
                            e.activation(dtmp[:, 128 * ii:128 * ii + 128],
                                         b[:, 128 * ii:128 * ii + 128], AF.Exp, scale=ISCALE))
                    e_done[i] = sS.n
                    rec.wait("vector", sS, e_done[i])
                    if i == 0:
                        rec.wait("vector", sPE, t_exp_pe)
                    mask_done[i] = rec.inc("vector", sV, lambda e, ii=i:
                                           e.tensor_tensor(expS[:, TRI[ii]:TRI[ii] + 128],
                                                           dtmp[:, 128 * ii:128 * ii + 128], trimask[:], op=ALU.mult))

                flush_attT_evac()

                rec.wait("tensor", sV, v_sigk)
                rec.wait("tensor", sS, mb_ready)
                rec.wait("tensor", sS, e_done[2])
                a_skm01 = None
                for i in range(4):
                    bank = pfA_banks[i % 2]
                    if i == 3:
                        rec.wait("tensor", sS, e_done[3])
                    cols = 130 * (i // 2)
                    fn = (lambda e, b=bank, ii=i, c=cols, mb=Mb, hp=h % 2:
                          e.matmul(b[:, c:c + 130], sigkT[:, 512 * hp + 128 * ii:512 * hp + 128 * ii + 128], mb[:],
                                   start=True, stop=True))
                    if i in (1, 3):
                        val = rec.inc("tensor", sPE, fn)
                        if i == 1:
                            a_skm01 = val
                        else:
                            a_skm23 = val
                    else:
                        rec.em("tensor", fn)

                rec.wait("tensor", sV, v_sigk)
                rec.wait("tensor", sS, t_pgB["s2"])
                rec.wait("tensor", sS, t_pgB["s"])
                for i in range(4):
                    fn = lambda e, ii=i, hp=h % 2: e.transpose(pgB[:, 128 * ii:128 * ii + 128],
                                                     sigkT[:, 512 * hp + 128 * ii:512 * hp + 128 * ii + 128], ident[:])
                    if i == 3:
                        a_trk = rec.inc("tensor", sPE, fn)
                    else:
                        rec.em("tensor", fn)
                rec.wait("scalar", sPE, a_trk)
                rec.inc("scalar", sS, lambda e: e.copy(sigkN[:], pgB[:, 0:512]))
                t_pgB["s2"] = sS.n
                sigkN_ev_s = sS.n

                rec.wait("vector", sPE, a_skm01)
                rec.wait("vector", sPE, t_delta_pe)
                for i in range(4):
                    if i == 2:
                        rec.wait("vector", sPE, a_skm23)
                    bank = pfA_banks[i % 2]
                    zc = 130 * (i // 2) + 129
                    fn = lambda e, b=bank, c=zc, ii=i: e.reciprocal(rcol[:, ii:ii + 1], b[:, c:c + 1])
                    if i == 3:
                        rec.inc("vector", sV, fn)
                    else:
                        rec.em("vector", fn)
                rec.wait("vector", sV, sV.n)
                for i in range(4):
                    bank = pfA_banks[i % 2]
                    vc = (4 * h + i) * 130
                    rec.inc("vector", sV, lambda e, b=bank, c=130 * (i // 2), ii=i, v=vc:
                            e.scalar_tensor_tensor(delta[:, 130 * ii:130 * ii + 128], b[:, c:c + 128],
                                                   rcol[:, ii:ii + 1], BIG0[:, v:v + 128],
                                                   op0=ALU.mult, op1=ALU.add))
                delta_done = sV.n
                t_pfA["v"] = delta_done
                t_pfF["v"] = delta_done
                t_pfA["s"] = e_done[2]
                t_pfF["s"] = e_done[3]

                rec.wait("tensor", sV, delta_done)
                rec.wait("tensor", sS, sigkN_ev_s)
                for i in range(4):
                    fn = (lambda e, ii=i: e.matmul(pfA_banks[1][:, 260:390], sigkN[:, 128 * ii:128 * ii + 128],
                                                   delta[:, 130 * ii:130 * ii + 130],
                                                   start=(ii == 0), stop=(ii == 3)))
                    if i == 3:
                        a_upd = rec.inc("tensor", sPE, fn)
                    else:
                        rec.em("tensor", fn)
                rec.wait("vector", sPE, a_upd)
                rec.inc("vector", sV, lambda e, mf=Mf: e.tensor_tensor(mf[:, 0:130], mf[:, 0:130], pfA_banks[1][:, 260:390], op=ALU.add))
                m_upd = sV.n
                t_pfF["v"] = max(t_pfF["v"], m_upd)

                rec.wait("tensor", sV, v_sigq)
                rec.wait("tensor", sV, t_pfB["v"])
                rec.wait("tensor", sS, t_pfB["s"])
                for j in range(4):
                    bank = pfB_ if j < 2 else pfC_
                    if j == 2:
                        rec.wait("tensor", sV, t_pfC["v"])
                        rec.wait("tensor", sS, t_pfC["s"])
                    cols = 130 * (j % 2)
                    fn = (lambda e, b=bank, jj=j, c=cols, mb=Mb, hp=h % 2:
                          e.matmul(b[:, c:c + 130], sigqT[:, 512 * hp + 128 * jj:512 * hp + 128 * jj + 128], mb[:],
                                   start=True, stop=True))
                    if j == 1:
                        a_am01 = rec.inc("tensor", sPE, fn)
                    elif j == 3:
                        a_am23 = rec.inc("tensor", sPE, fn)
                    else:
                        rec.em("tensor", fn)

                rec.wait("tensor", sV, mask_done[3])
                for j in range(4):
                    bank = pfD_ if j < 2 else pfE_
                    if j == 0:
                        rec.wait("tensor", sV, t_pfD["v"])
                        rec.wait("tensor", sS, t_pfD["s"])
                    if j == 2:
                        rec.wait("tensor", sV, t_pfE["v"])
                        rec.wait("tensor", sS, t_pfE["s"])
                    cols = 130 * (j % 2)
                    for i in range(j + 1):
                        vc = (4 * h + i) * 130
                        fn = (lambda e, b=bank, jj=j, ii=i, c=cols, v=vc:
                              e.matmul(b[:, c:c + 130],
                                       expS[:, TRI[ii] + 128 * (jj - ii):TRI[ii] + 128 * (jj - ii) + 128],
                                       BIG0[:, v:v + 130],
                                       start=(ii == 0), stop=(ii == jj)))
                        if j == 1 and i == 1:
                            a_ad01 = rec.inc("tensor", sPE, fn)
                        elif j == 3 and i == 3:
                            a_ad23 = rec.inc("tensor", sPE, fn)
                        else:
                            rec.em("tensor", fn)
                t_exp_pe = a_ad23
                t_sig_pe = a_am23
                t_delta_pe = a_upd

                rec.wait("vector", sPE, a_am01)
                for j in range(4):
                    if j == 2:
                        rec.wait("vector", sPE, a_am23)
                    bank = pfB_ if j < 2 else pfC_
                    zc = 130 * (j % 2) + 128
                    rec.inc("vector", sV, lambda e, b=bank, c=zc, jj=j: e.reciprocal(rcol[:, 4 + jj:5 + jj], b[:, c:c + 1]))
                rec.wait("vector", sPE, a_ad01)
                for j in range(4):
                    if j == 2:
                        rec.wait("vector", sPE, a_ad23)
                    bank = pfD_ if j < 2 else pfE_
                    zc = 130 * (j % 2) + 128
                    rec.inc("vector", sV, lambda e, b=bank, c=zc, jj=j: e.reciprocal(rcol[:, 8 + jj:9 + jj], b[:, c:c + 1]))
                recips = sV.n


                mb_upd_gate[h] = m_upd

                rec.wait("vector", sV, sV.n)
                rec.wait("vector", sPE, t_attN_pe)
                for j in range(4):
                    bad = pfD_ if j < 2 else pfE_
                    bam = pfB_ if j < 2 else pfC_
                    cols = 130 * (j % 2)
                    rec.em("vector", lambda e, b=bad, c=cols, jj=j:
                           e.tensor_scalar(adn[:, 128 * jj:128 * jj + 128], b[:, c:c + 128],
                                           rcol[:, 8 + jj:9 + jj], None, op0=ALU.mult))
                    rec.em("vector", lambda e, b=bam, c=cols, jj=j:
                           e.scalar_tensor_tensor(amn[:, 128 * jj:128 * jj + 128], b[:, c:c + 128],
                                                  rcol[:, 4 + jj:5 + jj], adn[:, 128 * jj:128 * jj + 128],
                                                  op0=ALU.mult, op1=ALU.subtract))
                    rec.em("vector", lambda e, jj=j, hh=h: e.tensor_tensor(amn[:, 128 * jj:128 * jj + 128],
                                                                           amn[:, 128 * jj:128 * jj + 128],
                                                                           betab[:, 128 * hh:128 * hh + 128], op=ALU.mult))
                    fn = (lambda e, jj=j: e.tensor_tensor(attN[:, 128 * jj:128 * jj + 128],
                                                          adn[:, 128 * jj:128 * jj + 128],
                                                          amn[:, 128 * jj:128 * jj + 128], op=ALU.add))
                    if j == 3:
                        comb_done = rec.inc("vector", sV, fn)
                    else:
                        rec.em("vector", fn)
                t_amn_v = comb_done
                emit_b_sigma_stt()
                t_rcol_s = comb_done
                t_pfB["s"] = 0
                t_pfC["s"] = 0
                t_pfD["s"] = 0
                t_pfE["s"] = 0
                t_pfB["v"] = comb_done
                t_pfC["v"] = comb_done
                t_pfD["v"] = comb_done
                t_pfE["v"] = comb_done

                pga = pgB if h % 2 == 0 else pgA
                tra = t_pgB if h % 2 == 0 else t_pgA
                rec.wait("tensor", sV, comb_done)
                if h % 2 == 0:
                    rec.wait("tensor", sS, sigkN_ev_s)
                rec.wait("tensor", sS, tra["s"])
                for j in range(4):
                    fn = lambda e, jj=j, pg_=pga: e.transpose(pg_[:, 512 + 128 * jj:512 + 128 * jj + 128],
                                                              attN[:, 128 * jj:128 * jj + 128], ident[:])
                    if j == 3:
                        a_attT = rec.inc("tensor", sPE, fn)
                    else:
                        rec.em("tensor", fn)
                t_attN_pe = a_attT
                attT_pend[0] = (h, a_attT)



        pending = []

        def drain_pending(k):
            for _ in range(k):
                if pending:
                    pending.pop(0)()

        def emit_c_op(seg):
            nonlocal op_prev_pe, h_tr_prev, t_h2T_pe
            if True:
                def ldwoq(e, cc=0):
                    src_ = wo_bf[:, 512 * cc:512 * (cc + 1)].rearrange("(b p) c -> p b c", p=128)
                    dst = BIGA[:, woq_off[cc % 2]:woq_off[cc % 2] + 8192].rearrange("p (b c) -> p b c", c=512)
                    return e.dma_start(out=dst, in_=src_)
                for c in range(2):
                    rec.wait("sync", sPE, woq_pe[c % 2])
                    rec.wait("sync", sGp, wo_cast)
                    woq_dma[c] = rec.inc("sync", sSy, lambda e, cc=c: ldwoq(e, cc), 16)

                attT_view = BIG0[:, 8320:8320 + 8192].rearrange("p (h x b) -> p h x b", h=16, b=16)
                opg = 0
                h2T_ready[:] = [0, 0, 0, 0]
                for c in range(4):
                    for r in range(4):
                        if c == 0:
                            rec.wait("sync", sOut, out_dma[r])
                            if seg == 0 and r >= 2:
                                rec.wait("sync", sS, wb_scale_done)
                            gr = 4 * seg + r
                            rec.inc("sync", sSy, lambda e, g=gr, sl=r:
                                    e.dma_start(out=x_sb[:, sl * D:sl * D + D], in_=x_h[128 * g:128 * (g + 1), :]), 16)
                            x_ld[r] = sSy.n
                        opbanks = (pfD_, pfE_, pfB_, pfC_)
                        optrk = (t_pfD, t_pfE, t_pfB, t_pfC)
                        bank = opbanks[opg % 4]
                        tb = optrk[opg % 4]
                        rec.wait("tensor", sSy, woq_dma[c])
                        rec.wait("tensor", sS, attT_ready[4 * r + 3])
                        rec.wait("tensor", sV, tb["v"])
                        rec.wait("tensor", sS, tb["s"])
                        for b in range(16):
                            fn = (lambda e, bb=b, rr=r, cc=c, bk=bank:
                                  e.matmul(bk[:], attT_view[:, 4 * rr:4 * rr + 4, :, bb:bb + 1],
                                           BIGA[:, woq_off[cc % 2] + 512 * bb:woq_off[cc % 2] + 512 * bb + 512],
                                           start=(bb == 0), stop=(bb == 15)))
                            if b == 15:
                                a_op = rec.inc("tensor", sPE, fn)
                            else:
                                rec.em("tensor", fn)
                        rec.wait("vector", sPE, a_op)
                        rec.wait("vector", sSy, x_ld[r])
                        rec.inc("vector", sV, lambda e, rr=r, cc=c, bk=bank:
                                e.tensor_tensor(x_sb[:, rr * D + 512 * cc:rr * D + 512 * cc + 512],
                                                bk[:], x_sb[:, rr * D + 512 * cc:rr * D + 512 * cc + 512],
                                                op=ALU.add))
                        optrk[opg % 4]["v"] = sV.n
                        opg += 1
                        if c == 3:
                            x2_done = sV.n
                            ln_done = ln_tile(x_sb[:, r * D:r * D + D], h_sb[:], sV, x2_done,
                                              extra_scalar_waits=((sPE, h_tr_prev),))
                            for w in range(2):
                                rec.wait("tensor", sS, ln_done)
                                rec.wait("tensor", sV, t_pgA["v"])
                                rec.wait("tensor", sS, t_pgA["s"])
                                for kk in range(8):
                                    k2 = 8 * w + kk
                                    fn = lambda e, kq=k2, co=128 * kk: e.transpose(pgA[:, co:co + 128],
                                                                                   h_sb[:, 128 * kq:128 * kq + 128], ident[:])
                                    if kk == 7:
                                        a_h2 = rec.inc("tensor", sPE, fn)
                                    else:
                                        rec.em("tensor", fn)
                                rec.wait("vector", sPE, a_h2)
                                rec.wait("vector", sPE, t_h2T_pe)
                                rec.inc("vector", sV, lambda e, rr=r, ww=w:
                                        e.tensor_copy(BIG0[:, h2T_off + 2048 * rr + 1024 * ww:h2T_off + 2048 * rr + 1024 * ww + 1024],
                                                      pgA[:]))
                                t_pgA["v"] = sV.n
                            h_tr_prev = sPE.n
                            h2T_ready[r] = t_pgA["s"]
                    woq_pe[c % 2] = sPE.n
                    if c + 2 < 4:
                        rec.wait("sync", sPE, woq_pe[c % 2])
                        woq_dma[c + 2] = rec.inc("sync", sSy, lambda e, cc=c + 2: ldwoq(e, cc), 16)
                op_prev_pe = sPE.n


        def queue_c_ffn(seg):
            h2T_rd = list(h2T_ready)
            def ldwtq(e, qq=0):
                src_ = wtT_bf[:, 256 * qq:256 * (qq + 1)].rearrange("(k p) c -> p k c", p=128)
                if qq % 2 == 0:
                    dst = BIG0[:, wtq_off:wtq_off + 4096].rearrange("p (k c) -> p k c", c=256)
                else:
                    dst = stage[:, 0:4096].rearrange("p (k c) -> p k c", c=256)
                return e.dma_start(out=dst, in_=src_)
            for q in range(2):
                rec.wait("sync", sPE, wtq_pe[q % 2])
                rec.wait("sync", sSy, wt_built)
                wtq_dma[q] = rec.inc("sync", sSy, lambda e, qq=q: ldwtq(e, qq), 16)
            out_last = [0, 0, 0, 0]
            ffg = [0]

            def emit_ffn_q(q, rlo, rhi):
                if q == 0 and rlo == 0:
                    rec.wait("tensor", sGp, bias_ld)
                for r in range(rlo, rhi):
                    bank = pfB_ if ffg[0] % 2 == 0 else pfC_
                    tb = t_pfB if ffg[0] % 2 == 0 else t_pfC
                    rec.wait("tensor", sSy, wtq_dma[q])
                    rec.wait("tensor", sV, tb["v"])
                    rec.wait("tensor", sS, max(tb["s"], h2T_rd[r]))
                    for k in range(ND):
                        if q % 2 == 0:
                            ws = BIG0[:, wtq_off + 256 * k:wtq_off + 256 * k + 256]
                        else:
                            ws = stage[:, 256 * k:256 * k + 256]
                        fn = (lambda e, kk2=k, rr=r, w2=ws, bk=bank:
                              e.matmul(bk[:, 0:256], BIG0[:, h2T_off + 2048 * rr + 128 * kk2:h2T_off + 2048 * rr + 128 * kk2 + 128],
                                       w2, start=(kk2 == 0), stop=False))
                        rec.em("tensor", fn)
                    a_ff = rec.inc("tensor", sPE, lambda e, qq=q, bk=bank:
                                   e.matmul(bk[:, 0:256], trimask[0:1, :], biasr[0:1, 256 * qq:256 * qq + 256],
                                            start=False, stop=True))
                    rec.wait("vector", sPE, a_ff)
                    xoff = r * D + 256 * q
                    rec.inc("vector", sV, lambda e, xo=xoff, bk=bank: e.tensor_tensor(x_sb[:, xo:xo + 256], bk[:, 0:256],
                                                                                      x_sb[:, xo:xo + 256], op=ALU.add))
                    if ffg[0] % 2 == 0:
                        t_pfB["v"] = sV.n
                    else:
                        t_pfC["v"] = sV.n
                    out_last[r] = sV.n
                    ffg[0] += 1
                if rhi == 4:
                    wtq_pe[q % 2] = sPE.n
                    if q + 2 < 8:
                        rec.wait("sync", sPE, wtq_pe[q % 2])
                        wtq_dma[q + 2] = rec.inc("sync", sSy, lambda e, qq=q + 2: ldwtq(e, qq), 16)
                    if q == 7:
                        nonlocal t_h2T_pe
                        t_h2T_pe = sPE.n

            def emit_outs():
                for r in range(4):
                    gr = 4 * seg + r
                    rec.wait("sync", sV, out_last[r])
                    rec.inc("sync", sOut, lambda e, g=gr, sl=r:
                            e.dma_start(out=out_h[128 * g:128 * (g + 1), :], in_=x_sb[:, sl * D:sl * D + D]), 16)
                    out_dma[r] = sOut.n

            for q in range(8):
                pending.append(lambda qq=q: emit_ffn_q(qq, 0, 4))
            pending.append(emit_outs)

        for seg in range(NSEG):
            emit_b_seg_start(seg)
            emit_b_sigma(seg, 0)
            emit_b_sigma_stt()
            for h in range(H):
                emit_b_head(seg, h)
                if h >= 4:
                    drain_pending(1)
            flush_attT_evac()
            seg_pe_end = sPE.n
            seg_v_end = sV.n
            drain_pending(17)
            emit_c_op(seg)
            queue_c_ffn(seg)
        drain_pending(17)

        rec.wait("sync", sOut, sOut.n)
        rec.run(nc)

    nc.finalize()
    return nc


def make_consts():
    ident = np.eye(128, dtype=ml_dtypes.bfloat16)
    tri = np.triu(np.ones((128, 128), dtype=np.float32))
    trimask = tri.astype(ml_dtypes.bfloat16)
    return ident, trimask


def make_inputs(L0_attn_x, L0_attn_Wq, L0_attn_Wk, L0_attn_Wv, L0_attn_Wo, L0_attn_betas,
                L0_ffn_weight, L0_ffn_scale_weight, L0_ffn_bias):
    x = np.ascontiguousarray(np.asarray(L0_attn_x, dtype=np.float32))
    ident, trimask = make_consts()
    betas = np.asarray(L0_attn_betas, dtype=np.float32).reshape(H, DK)
    betab = np.ascontiguousarray(np.broadcast_to(betas.reshape(1, D), (128, D))).astype(np.float32)
    bias = np.asarray(L0_ffn_bias, dtype=np.float32)
    biasb = np.ascontiguousarray(np.broadcast_to(bias.reshape(1, D), (128, D))).astype(np.float32)
    scale = np.asarray(L0_ffn_scale_weight, dtype=np.float32)
    fscale = np.ascontiguousarray(scale.reshape(ND, 128).T).astype(np.float32)
    base = {"wq": np.ascontiguousarray(np.asarray(L0_attn_Wq, np.float32)),
            "wk": np.ascontiguousarray(np.asarray(L0_attn_Wk, np.float32)),
            "wv": np.ascontiguousarray(np.asarray(L0_attn_Wv, np.float32)),
            "wo": np.ascontiguousarray(np.asarray(L0_attn_Wo, np.float32)),
            "wf": np.ascontiguousarray(np.asarray(L0_ffn_weight, np.float32)),
            "ident": ident, "trimask": trimask, "betab": betab, "biasb": biasb,
            "fscale": fscale}
    return [dict(base, x=np.ascontiguousarray(x[b])) for b in range(8)]


def kernel(**inputs):
    in_maps = make_inputs(**inputs)
    nc = build_program()
    res = run_bass_kernel_spmd(nc, in_maps, core_ids=list(range(8)))
    return np.stack([r["out"] for r in res.results], axis=0)
